# version 11
# baseline (speedup 1.0000x reference)
import numpy as np
from contextlib import ExitStack

import concourse.bass as bass
import concourse.mybir as mybir
from concourse.bass_utils import run_bass_kernel_spmd

F32 = mybir.dt.float32
BF16 = mybir.dt.bfloat16
AF = mybir.ActivationFunctionType
ALU = mybir.AluOpType
AX = mybir.AxisListType

T = 2048
C = 2048
HKV = 8
HQ = 16
D = 128
KVD = 1024
TC = 16
CC = 16
NSTRIP = 4
GATE_OFFSET = 6.906768
LN128 = float(np.log(128.0))

E = 32
I_FF = 1408
IC = 11
SI = 2816
SIC = 22
RK = 128
NBLK = 4

ENGINES = ("tensor", "vector", "scalar", "sync", "gpsimd")


class Sem:
    __slots__ = ("h", "v", "name")

    def __init__(self, nc, ctx, name):
        self.h = ctx.enter_context(nc.semaphore(name))
        self.v = 0
        self.name = name


class Rec:
    def __init__(self):
        self.q = {e: [] for e in ENGINES}

    def op(self, eng, make, waits=(), inc=None, incn=1):
        wl = [(s.h, v) for (s, v) in waits]

        def run(e, make=make, wl=wl, inc=inc, incn=incn):
            for h, v in wl:
                e.wait_ge(h, v)
            ins = make(e)
            if inc is not None:
                ins.then_inc(inc.h, incn)
        self.q[eng].append(run)
        if inc is not None:
            inc.v += incn

    def wait(self, eng, sem, v=None):
        vv = sem.v if v is None else v
        self.q[eng].append(lambda e, h=sem.h, vv=vv: e.wait_ge(h, vv))

    def flush(self, block):
        for name in ENGINES:
            fns = self.q[name]
            if fns:
                getattr(block, name)(lambda eng, fns=list(fns): [f(eng) for f in fns])
        self.q = {e: [] for e in ENGINES}


def build_core_program(stage="full", dump=()):
    nc = bass.Bass("TRN2", target_bir_lowering=False)
    ctx = ExitStack()

    def din(name, shape):
        return nc.dram_tensor(name, list(shape), F32, kind="ExternalInput").ap()

    x_in = din("x", (T, C))
    Wq = din("Wq", (C, C)); Wk = din("Wk", (C, KVD)); Wv = din("Wv", (C, KVD))
    Wg = din("Wg", (C, HKV)); Wc = din("Wc", (C, C)); bc_in = din("bc", (1, C))
    gw = din("gw", (E, C))
    Rg = din("Rg", (RK, C)); Ru = din("Ru", (RK, C)); Rd = din("Rd", (RK, I_FF))
    Ug = din("Ug", (E, RK, RK)); Uu = din("Uu", (E, RK, RK)); Ud = din("Ud", (E, RK, RK))
    Cg = din("Cg", (E, I_FF, RK)); Cu = din("Cu", (E, I_FF, RK)); Cd = din("Cd", (E, C, RK))
    sRg = din("sRg", (RK, C)); sRu = din("sRu", (RK, C)); sRd = din("sRd", (RK, SI))
    sUg = din("sUg", (RK, RK)); sUu = din("sUu", (RK, RK)); sUd = din("sUd", (RK, RK))
    sCg = din("sCg", (SI, RK)); sCu = din("sCu", (SI, RK)); sCd = din("sCd", (C, RK))

    out_t = nc.dram_tensor("out", [T, C], F32, kind="ExternalOutput").ap()

    def scratch(name, shape, dtype=BF16):
        kind = "ExternalOutput" if name in dump else "Internal"
        return nc.dram_tensor(name, list(shape), dtype, kind=kind).ap()
    dbg = {}
    def dump_sbuf(tag, sb_ap, waits=()):
        if tag not in dump:
            return
        shp = list(sb_ap.shape)
        dbg[tag] = nc.dram_tensor("dbg_" + tag, shp, sb_ap.dtype,
                                  kind="ExternalOutput").ap()
        rec.op("sync", lambda sy, d=dbg[tag], s=sb_ap: sy.dma_start(out=d[:, :], in_=s),
               waits=waits, inc=S["dmaO"], incn=16)
    xbf = scratch("xbf", (CC, T, 128))
    qsp = scratch("qsp", (HQ, D, T))
    xf32 = scratch("xf32", (T, C), F32)
    xfbf = scratch("xfbf", (CC, T, 128))
    bUUU = scratch("bUUU", (E, 3 * RK, RK))
    bCg = scratch("bCg", (E, I_FF, RK))
    bCu = scratch("bCu", (E, I_FF, RK))
    bCd = scratch("bCd", (E, C, RK))
    bRg = scratch("bRg", (RK, C)); bRu = scratch("bRu", (RK, C)); bRd = scratch("bRd", (RK, I_FF))
    bgw = scratch("bgw", (E, C))
    bsRg = scratch("bsRg", (RK, C)); bsRu = scratch("bsRu", (RK, C)); bsRd = scratch("bsRd", (RK, SI))
    bsUUU = scratch("bsUUU", (3 * RK, RK))
    bsCg = scratch("bsCg", (SI, RK)); bsCu = scratch("bsCu", (SI, RK)); bsCd = scratch("bsCd", (C, RK))
    normsp = scratch("normsp", (16, T), F32)
    bUUUT = scratch("bUUUT", (E, 128, 384))
    bCgT = scratch("bCgT", (E, 128, I_FF))
    bCuT = scratch("bCuT", (E, 128, I_FF))
    bCdT = scratch("bCdT", (E, 4, 128, 512))

    ident = ctx.enter_context(nc.sbuf_tensor("ident", [128, 128], F32, side="right"))
    band = ctx.enter_context(nc.sbuf_tensor("band", [128, 896], BF16, side="right"))
    ones_col = ctx.enter_context(nc.sbuf_tensor("ones_col", [128, 1], BF16, side="right"))
    ones_row = ctx.enter_context(nc.sbuf_tensor("ones_row", [1, 128], BF16, side="right"))
    a2h_tm = ctx.enter_context(nc.sbuf_tensor("a2h_tm", [128, TC * HKV], F32, side="right"))
    a2q_tm = ctx.enter_context(nc.sbuf_tensor("a2q_tm", [128, TC * HKV], F32, side="right"))
    egam = ctx.enter_context(nc.sbuf_tensor("egam", [8, T], F32, side="right"))
    gconst = ctx.enter_context(nc.sbuf_tensor("gconst", [128, 6], F32, side="right"))
    dup8 = ctx.enter_context(nc.sbuf_tensor("dup8", [8, 16], F32, side="right"))
    ident_bf = ctx.enter_context(nc.sbuf_tensor("ident_bf", [128, 128], BF16, side="right"))

    bstg = [ctx.enter_context(nc.sbuf_tensor(f"bstg{i}", [128, C], BF16, side="right"))
            for i in range(2)]
    pool = [ctx.enter_context(nc.psum_tensor(f"pp{i}", [128, 512], F32)) for i in range(8)]

    S = {}
    for n in ("bar", "dmaC", "dmaX", "dmaW", "dmaQ", "dmaS", "dmaF", "dmaO", "dmaR",
              "tKV", "tG", "tQ", "tS", "tYn", "tQf", "tWf", "tO", "tTr", "tCd",
              "sAe", "sLN", "sFF", "bTl", "bSt",
              "vEv", "vAm", "vAe", "vLN", "vPd", "vTk", "vFx", "vZ2"):
        S[n] = Sem(nc, ctx, n)
    prel = [Sem(nc, ctx, f"prel{i}") for i in range(8)]
    puse = [0] * 8

    rec = Rec()
    blockH = ctx.enter_context(nc.Block())

    def ps_free(i, eng="tensor"):
        rec.wait(eng, prel[i], puse[i])

    def ps_rel(i):
        puse[i] += 1
        return prel[i]

    def bar():
        base = S["bar"].v
        for eng in ENGINES:
            rec.op(eng, lambda e: e.nop(), inc=S["bar"])
        for eng in ENGINES:
            rec.wait(eng, S["bar"], base + 5)

    warm_out = ctx.enter_context(nc.sbuf_tensor("warm_out", [1, 4], F32, side="right"))

    def warm(func, n=2):
        for _ in range(n):
            rec.op("scalar", lambda s, func=func: s.activation(
                warm_out[0:1, 0:1], gconst[0:1, 0:1], func))

    def mm512(ps_list, stat, mov, start, stop, waits=(), inc=None, rows=128):
        n = len(ps_list)
        for k in range(n):
            rec.op("tensor",
                   lambda t, o=ps_list[k], s=stat, m=mov[:, 512 * k:512 * (k + 1)],
                          st=start, sp=stop, rows=rows: t.matmul(
                       o[0:rows, :], s, m, start=st, stop=sp),
                   waits=(waits if k == 0 else ()),
                   inc=(inc if k == n - 1 else None))

    p0ctx = ExitStack()
    iota_f = p0ctx.enter_context(nc.sbuf_tensor("iota_f", [128, 896], F32))
    rec.op("gpsimd", lambda g: g.iota(iota_f[:, :], pattern=[[1, 896]], base=-384,
                                      channel_multiplier=-1,
                                      allow_small_or_imprecise_dtypes=True), inc=S["dmaC"])
    rec.op("vector", lambda v: v.tensor_single_scalar(band[:, :], iota_f[:, :], 0.0,
                                                      ALU.is_ge),
           waits=[(S["dmaC"], S["dmaC"].v)], inc=S["vEv"])
    rec.op("gpsimd", lambda g: g.iota(iota_f[:, 0:128], pattern=[[1, 128]], base=0,
                                      channel_multiplier=-1,
                                      allow_small_or_imprecise_dtypes=True),
           waits=[(S["vEv"], S["vEv"].v)], inc=S["dmaC"])
    rec.op("vector", lambda v: v.tensor_single_scalar(ident[:, :], iota_f[:, 0:128], 0.0,
                                                      ALU.is_equal),
           waits=[(S["dmaC"], S["dmaC"].v)], inc=S["vEv"])
    rec.wait("vector", S["vEv"], S["vEv"].v)
    rec.op("vector", lambda v: v.tensor_copy(ident_bf[:, :], ident[:, :]), inc=S["vEv"])
    rec.wait("vector", S["vEv"], S["vEv"].v)
    rec.op("gpsimd", lambda g: g.iota(dup8[:, :], pattern=[[-1, 8], [0, 2]],
                                      base=0, channel_multiplier=1,
                                      allow_small_or_imprecise_dtypes=True),
           inc=S["dmaC"])
    rec.op("vector", lambda v: v.tensor_single_scalar(dup8[:, :], dup8[:, :], 0.0,
                                                      ALU.is_equal),
           waits=[(S["dmaC"], S["dmaC"].v)], inc=S["vEv"])
    rec.op("vector", lambda v: v.memset(ones_col[:, :], 1.0), inc=S["vEv"])
    rec.op("vector", lambda v: v.memset(ones_row[:, :], 1.0), inc=S["vEv"])
    rec.op("vector", lambda v: v.memset(gconst[:, 0:1], GATE_OFFSET), inc=S["vEv"])
    rec.op("vector", lambda v: v.memset(gconst[:, 1:2], -0.5), inc=S["vEv"])
    rec.op("vector", lambda v: v.memset(gconst[:, 2:3], -0.5 * LN128), inc=S["vEv"])
    rec.op("vector", lambda v: v.memset(gconst[:, 3:4], 0.5), inc=S["vEv"])
    rec.op("vector", lambda v: v.memset(gconst[:, 4:5], -1.0), inc=S["vEv"])
    rec.op("vector", lambda v: v.memset(gconst[:, 5:6], -LN128), inc=S["vEv"])
    ident_ready = S["vEv"].v




    p0ctx.close()

    a2s = ExitStack()
    K_sb = a2s.enter_context(nc.sbuf_tensor("K_sb", [128, HKV * T], BF16))
    V_sb = a2s.enter_context(nc.sbuf_tensor("V_sb", [128, TC * KVD], BF16))

    a1 = ExitStack()
    x_fm = a1.enter_context(nc.sbuf_tensor("x_fm", [128, CC * T], BF16))
    wbig = a1.enter_context(nc.sbuf_tensor("wbig", [128, CC * 512], BF16))
    wq_sb = [a1.enter_context(nc.sbuf_tensor(f"wq{i}", [128, CC * 2 * D], BF16)) for i in range(2)]
    qtile = [a1.enter_context(nc.sbuf_tensor("qt0", [128, T], BF16))] * 2
    gtmp = a1.enter_context(nc.sbuf_tensor("gtmp", [8, T], F32, side="right"))
    gbuf = a1.enter_context(nc.sbuf_tensor("gbuf", [8, T], F32))
    a2h_fm = gtmp
    a2q_fm = wq_sb[1][:, :].bitcast(F32)[0:8, :]

    xtm = [wbig[:, 0:C], wbig[:, C:2 * C]]
    xtm_marks = {}
    for tc in range(TC):
        w = [(S["tTr"], xtm_marks[tc - 2])] if tc >= 2 else []
        rec.op("gpsimd", lambda g, tc=tc: g.dma_start(
            out=xtm[tc % 2], in_=x_in[tc * 128:(tc + 1) * 128, :]),
            waits=w, inc=S["dmaX"], incn=16)
        ld_mark = S["dmaX"].v
        for cc in range(CC):
            i = (tc * CC + cc) % 8
            ps_free(i)
            w2 = [(S["dmaX"], ld_mark)] if cc == 0 else []
            if tc == 0 and cc == 0:
                w2.append((S["vEv"], ident_ready))
            rec.op("tensor", lambda t, tc=tc, cc=cc, i=i: t.matmul(
                pool[i][:, 0:128], xtm[tc % 2][:, cc * 128:(cc + 1) * 128],
                ident_bf[:, :], start=True, stop=True),
                waits=w2, inc=S["tTr"])
            rec.op("vector", lambda v, tc=tc, cc=cc, i=i: v.tensor_copy(
                x_fm[:, cc * T + tc * 128: cc * T + (tc + 1) * 128], pool[i][:, 0:128]),
                waits=[(S["tTr"], S["tTr"].v)], inc=ps_rel(i))
        xtm_marks[tc] = S["tTr"].v
    rec.op("vector", lambda v: v.engine_nop(), inc=S["vEv"])
    xfm_ready = S["vEv"].v
    def cast_dma(dst, src):
        rec.op("gpsimd", lambda g, d=dst, s=src: g.dma_start(out=d, in_=s),
               inc=S["dmaC"], incn=16)
    cast_dma(bUUU[:, 0 * RK:1 * RK, :], Ug[:, :, :])
    cast_dma(bUUU[:, 1 * RK:2 * RK, :], Uu[:, :, :])
    cast_dma(bUUU[:, 2 * RK:3 * RK, :], Ud[:, :, :])
    cast_dma(bCg[:, :, :], Cg[:, :, :])
    cast_dma(bCu[:, :, :], Cu[:, :, :])
    cast_dma(bCd[:, :, :], Cd[:, :, :])
    cast_dma(bRg[:, :], Rg[:, :])
    cast_dma(bRu[:, :], Ru[:, :])
    cast_dma(bRd[:, :], Rd[:, :])
    cast_dma(bgw[:, :], gw[:, :])
    cast_dma(bsRg[:, :], sRg[:, :])
    cast_dma(bsRu[:, :], sRu[:, :])
    cast_dma(bsRd[:, :], sRd[:, :])
    cast_dma(bsUUU[0 * RK:1 * RK, :], sUg[:, :])
    cast_dma(bsUUU[1 * RK:2 * RK, :], sUu[:, :])
    cast_dma(bsUUU[2 * RK:3 * RK, :], sUd[:, :])
    cast_dma(bsCg[:, :], sCg[:, :])
    cast_dma(bsCu[:, :], sCu[:, :])
    cast_dma(bsCd[:, :], sCd[:, :])
    cast_total = S["dmaC"].v


    for kh in range(2):
        rec.op("gpsimd", lambda g, kh=kh: g.dma_start(
            out=wbig[:, :].rearrange("p (cc f) -> p cc f", f=512),
            in_=Wk[:, kh * 512:(kh + 1) * 512].rearrange("(cc p) f -> p cc f", p=128)),
            waits=([(S["tKV"], S["tKV"].v)] if kh > 0
                   else [(S["tTr"], xtm_marks[TC - 1])]),
            inc=S["dmaW"], incn=16)
        wk_done = S["dmaW"].v
        for mm_ in range(4):
            m = kh * 4 + mm_
            q0 = 4 * (m % 2)
            for i in range(q0, q0 + 4):
                ps_free(i)
            for cc in range(CC):
                stat = wbig[:, cc * 512 + mm_ * D: cc * 512 + (mm_ + 1) * D]
                mm512([pool[q0], pool[q0 + 1], pool[q0 + 2], pool[q0 + 3]], stat,
                      x_fm[:, cc * T:(cc + 1) * T],
                      start=(cc == 0), stop=(cc == CC - 1),
                      waits=([(S["dmaW"], wk_done), (S["vEv"], xfm_ready)] if cc == 0 else ()),
                      inc=(S["tKV"] if cc == CC - 1 else None))
            for i in range(4):
                rec.op("vector", lambda v, m=m, i=i, q0=q0: v.tensor_copy(
                    K_sb[:, m * T + i * 512: m * T + (i + 1) * 512], pool[q0 + i][:, :]),
                    waits=([(S["tKV"], S["tKV"].v)] if i == 0 else ()), inc=ps_rel(q0 + i))

    for fh in range(2):
        rec.op("gpsimd", lambda g, fh=fh: g.dma_start(
            out=wbig[:, :].rearrange("p (cc f) -> p cc f", f=512),
            in_=Wv[:, fh * 512:(fh + 1) * 512].rearrange("(cc p) f -> p cc f", p=128)),
            waits=[(S["tKV"], S["tKV"].v)], inc=S["dmaW"], incn=16)
        wv_done = S["dmaW"].v
        for tc in range(TC):
            i = tc % 6
            ps_free(i)
            for cc in range(CC):
                stat = x_fm[:, cc * T + tc * 128: cc * T + (tc + 1) * 128]
                rec.op("tensor", lambda t, i=i, stat=stat, cc=cc: t.matmul(
                    pool[i][:, :], stat, wbig[:, cc * 512:(cc + 1) * 512],
                    start=(cc == 0), stop=(cc == CC - 1)),
                    waits=([(S["dmaW"], wv_done)] if cc == 0 else ()),
                    inc=(S["tKV"] if cc == CC - 1 else None))
            rec.op("vector", lambda v, tc=tc, fh=fh, i=i: v.tensor_copy(
                V_sb[:, tc * KVD + fh * 512: tc * KVD + (fh + 1) * 512], pool[i][:, :]),
                waits=[(S["tKV"], S["tKV"].v)], inc=ps_rel(i))

    rec.op("gpsimd", lambda g: g.dma_start(
        out=wq_sb[0][:, 0:CC * 8].rearrange("p (cc h) -> p cc h", h=8),
        in_=Wg.rearrange("(cc p) h -> p cc h", p=128)),
        inc=S["dmaW"], incn=16)
    wg_done = S["dmaW"].v
    for i in range(4, 8):
        ps_free(i)
    for cc in range(CC):
        stat = wq_sb[0][:, cc * 8:(cc + 1) * 8]
        mm512([pool[4], pool[5], pool[6], pool[7]], stat,
              x_fm[:, cc * T:(cc + 1) * T],
              start=(cc == 0), stop=(cc == CC - 1),
              waits=([(S["dmaW"], wg_done)] if cc == 0 else ()),
              inc=(S["tG"] if cc == CC - 1 else None), rows=8)
    for i in range(4):
        rec.op("vector", lambda v, i=i: v.tensor_copy(
            gtmp[:, i * 512:(i + 1) * 512], pool[4 + i][0:8, :]),
            waits=([(S["tG"], S["tG"].v)] if i == 0 else ()),
            inc=ps_rel(4 + i))
    rec.op("vector", lambda v: v.engine_nop(), inc=S["vEv"])
    g_ev = S["vEv"].v
    rec.op("scalar", lambda s: s.nop(), waits=[(S["vEv"], g_ev)])
    warm(AF.Sigmoid)
    rec.op("scalar", lambda s: s.activation(gbuf[:, :], gtmp[:, :], AF.Sigmoid,
                                            bias=gconst[0:8, 0:1]), inc=S["sFF"])
    warm(AF.Ln)
    rec.op("scalar", lambda s: s.activation(gtmp[:, :], gbuf[:, :], AF.Ln), inc=S["sFF"])
    lg_done = S["sFF"].v
    rec.op("vector", lambda v: v.tensor_tensor_scan(gbuf[:, :], gtmp[:, :], gtmp[:, :],
                                                    0.0, ALU.add, ALU.bypass),
           waits=[(S["sFF"], lg_done)], inc=S["vEv"])
    gam_done = S["vEv"].v
    rec.op("scalar", lambda s: s.nop(), waits=[(S["vEv"], gam_done)])
    warm(AF.Exp)
    rec.op("scalar", lambda s: s.activation(a2h_fm[:, :], gbuf[:, :], AF.Exp,
                                            scale=gconst[0:8, 1:2], bias=gconst[0:8, 2:3]),
           inc=S["sFF"])
    rec.op("scalar", lambda s: s.activation(egam[:, :], gbuf[:, :], AF.Exp), inc=S["sFF"])
    rec.op("scalar", lambda s: s.activation(a2q_fm[:, :], gbuf[:, :], AF.Exp,
                                            scale=gconst[0:8, 4:5], bias=gconst[0:8, 5:6]),
           inc=S["sFF"])
    a2h_done = S["sFF"].v
    for sc in range(TC):
        i = sc % 4
        ps_free(i)
        rec.op("tensor", lambda t, sc=sc, i=i: t.matmul(
            pool[i][:, 0:8], a2h_fm[:, sc * 128:(sc + 1) * 128], ident[0:8, 0:8],
            start=True, stop=True),
            waits=([(S["sFF"], a2h_done), (S["vEv"], ident_ready)] if sc == 0 else ()),
            inc=S["tTr"])
        rec.op("tensor", lambda t, sc=sc, i=i: t.matmul(
            pool[i][:, 8:16], a2q_fm[:, sc * 128:(sc + 1) * 128], ident[0:8, 0:8],
            start=True, stop=True), inc=S["tTr"])
        rec.op("vector", lambda v, sc=sc, i=i: v.tensor_copy(
            a2h_tm[:, sc * 8:(sc + 1) * 8], pool[i][:, 0:8]),
            waits=[(S["tTr"], S["tTr"].v)])
        rec.op("vector", lambda v, sc=sc, i=i: v.tensor_copy(
            a2q_tm[:, sc * 8:(sc + 1) * 8], pool[i][:, 8:16]), inc=ps_rel(i))

    for p in range(HQ // 2):
        rec.op("gpsimd", lambda g, p=p: g.dma_start(
            out=wq_sb[p % 2][:, :].rearrange("p (cc f) -> p cc f", f=2 * D),
            in_=Wq[:, p * 2 * D:(p + 1) * 2 * D].rearrange("(cc p) f -> p cc f", p=128)),
            waits=([(S["tQ"], 2 * (p - 1))] if p >= 2
                   else [(S["tG"], S["tG"].v), (S["tTr"], S["tTr"].v)]),
            inc=S["dmaW"], incn=16)
        wq_done = S["dmaW"].v
        for h in range(2):
            j = 2 * p + h
            q0 = 4 * (j % 2)
            for i in range(q0, q0 + 4):
                ps_free(i)
            for cc in range(CC):
                stat = wq_sb[p % 2][:, cc * 2 * D + h * D: cc * 2 * D + (h + 1) * D]
                mm512([pool[q0], pool[q0 + 1], pool[q0 + 2], pool[q0 + 3]], stat,
                      x_fm[:, cc * T:(cc + 1) * T],
                      start=(cc == 0), stop=(cc == CC - 1),
                      waits=([(S["dmaW"], wq_done)] if cc == 0 else ()),
                      inc=(S["tQ"] if cc == CC - 1 else None))
            for i in range(4):
                w = []
                if i == 0:
                    w.append((S["tQ"], S["tQ"].v))
                    if j >= 1:
                        w.append((S["dmaQ"], 16 * j))
                rec.op("vector", lambda v, j=j, i=i, q0=q0: v.tensor_copy(
                    qtile[j % 2][:, i * 512:(i + 1) * 512], pool[q0 + i][:, :]),
                    waits=w, inc=ps_rel(q0 + i))
            rec.op("vector", lambda v: v.engine_nop(), inc=S["vEv"])
            rec.op("sync", lambda sy, j=j: sy.dma_start(out=qsp[j, :, :],
                                                        in_=qtile[j % 2][:, :]),
                   waits=[(S["vEv"], S["vEv"].v)], inc=S["dmaQ"], incn=16)

    dump_sbuf("K", K_sb[:, :], waits=[(S["tKV"], S["tKV"].v)])
    dump_sbuf("V", V_sb[:, :], waits=[(S["tKV"], S["tKV"].v)])
    dump_sbuf("gam", gbuf[:, :], waits=[(S["sFF"], S["sFF"].v)])
    dump_sbuf("egam", egam[:, :], waits=[(S["sFF"], S["sFF"].v)])
    dump_sbuf("a2h_tm", a2h_tm[:, :], waits=[(S["vEv"], S["vEv"].v)])
    a1.close()
    bar()

    selctx = ExitStack()
    sel = selctx.enter_context(nc.sbuf_tensor("sel", [32, 32 * 128], BF16, side="right"))
    rec.op("gpsimd", lambda g: g.iota(sel[:, :], pattern=[[-1, 32], [0, 128]],
                                      base=0, channel_multiplier=1,
                                      allow_small_or_imprecise_dtypes=True),
           inc=S["dmaC"])
    rec.op("vector", lambda v: v.tensor_single_scalar(sel[:, :], sel[:, :], 0.0,
                                                      ALU.is_equal),
           waits=[(S["dmaC"], S["dmaC"].v)], inc=S["vEv"])
    rec.wait("vector", S["vEv"], S["vEv"].v)
    sel_ready = S["vEv"].v

    a_mid = ExitStack()
    ysb = a_mid.enter_context(nc.sbuf_tensor("ysb", [128, HQ * T], BF16, side="right"))
    a2q = ExitStack()
    qpair = [a2q.enter_context(nc.sbuf_tensor(f"qp{i}", [128, 2 * T], BF16))
             for i in range(2)]
    asb = [a2q.enter_context(nc.sbuf_tensor(f"asb{i}", [128, 1024], BF16, side="right"))
           for i in range(2)]
    norm_stage = a2q.enter_context(nc.sbuf_tensor("norm_stage", [1, 1024], F32))

    warm(AF.Square)
    nglob = 0
    diag_cnt = 0
    y_done = {}
    chunk_id = {}
    for m in range(HKV):
        rec.op("sync", lambda sy, m=m: sy.dma_start(out=qpair[m % 2][:, 0:T],
                                                    in_=qsp[2 * m, :, :]),
               waits=([(S["tQf"], m - 1)] if m >= 2 else ()), inc=S["dmaR"], incn=16)
        rec.op("sync", lambda sy, m=m: sy.dma_start(out=qpair[m % 2][:, T:2 * T],
                                                    in_=qsp[2 * m + 1, :, :]),
               inc=S["dmaR"], incn=16)
        qld = S["dmaR"].v
        for j in range(NSTRIP):
            nch = 4 * (j + 1)
            pend = []

            def emit_ynorm(info, last):
                c, par, wY, gidx = info
                vstat = V_sb[:, c * KVD + m * D: c * KVD + (m + 1) * D]
                if c == 0:
                    ps_free(4); ps_free(5); ps_free(6); ps_free(7)
                rec.op("tensor", lambda t, par=par, vstat=vstat, c=c, last=last: t.matmul(
                    pool[4][:, :], vstat, asb[par][:, 0:512],
                    start=(c == 0), stop=last), waits=wY)
                rec.op("tensor", lambda t, par=par, vstat=vstat, c=c, last=last: t.matmul(
                    pool[5][:, :], vstat, asb[par][:, 512:1024],
                    start=(c == 0), stop=last))
                rec.op("tensor", lambda t, par=par, c=c, last=last: t.matmul(
                    pool[6][0:1, :], ones_col[:, :], asb[par][:, 0:512],
                    start=(c == 0), stop=last))
                rec.op("tensor", lambda t, par=par, c=c, last=last: t.matmul(
                    pool[7][0:1, :], ones_col[:, :], asb[par][:, 512:1024],
                    start=(c == 0), stop=last), inc=S["tYn"])
                y_done[gidx] = S["tYn"].v

            for c in range(nch):
                par = nglob % 2
                kstat = K_sb[:, m * T + c * 128: m * T + (c + 1) * 128]
                w = [(S["sAe"], max(0, nglob - 1)), (S["vAe"], max(0, nglob - 1))]
                if c == 0:
                    w.append((S["dmaR"], qld))
                rec.op("tensor", lambda t, m=m, j=j, par=par, kstat=kstat: t.matmul(
                    pool[2 * par][:, :], kstat,
                    qpair[m % 2][:, j * 512:(j + 1) * 512], start=True, stop=True),
                    waits=w)
                rec.op("tensor", lambda t, m=m, j=j, par=par, kstat=kstat: t.matmul(
                    pool[2 * par + 1][:, :], kstat,
                    qpair[m % 2][:, T + j * 512: T + (j + 1) * 512], start=True, stop=True),
                    inc=S["tS"])
                wS = [(S["tS"], S["tS"].v),
                      (S["tYn"], y_done.get(nglob - 2, 0))]
                a2col = a2h_tm[:, c * 8 + m: c * 8 + m + 1]
                a2qcol = a2q_tm[:, c * 8 + m: c * 8 + m + 1]
                rec.op("scalar", lambda s, par=par, a2col=a2col: s.activation(
                    asb[par][:, 0:512], pool[2 * par][:, :], AF.Square, scale=a2col),
                    waits=wS, inc=S["sAe"])
                rec.op("scalar", lambda s, par=par, a2col=a2col: s.activation(
                    asb[par][:, 512:1024], pool[2 * par + 1][:, :], AF.Square,
                    scale=a2col), inc=S["vAe"])
                if c >= 4 * j:
                    kp = c - 4 * j
                    msk = band[:, 384 - 128 * kp: 896 - 128 * kp]
                    rec.op("vector", lambda v, par=par, msk=msk: v.tensor_mul(
                        asb[par][:, 0:512], asb[par][:, 0:512], msk),
                        waits=[(S["sAe"], S["sAe"].v)])
                    rec.op("vector", lambda v, par=par, msk=msk: v.tensor_mul(
                        asb[par][:, 512:1024], asb[par][:, 512:1024], msk),
                        inc=S["vAm"])
                    diag_cnt += 1
                    wY = [(S["vAm"], diag_cnt)]
                else:
                    wY = [(S["sAe"], S["sAe"].v), (S["vAe"], S["vAe"].v)]
                pend.append((c, par, wY, nglob))
                if len(pend) > 1:
                    emit_ynorm(pend.pop(0), last=False)
                nglob += 1
            emit_ynorm(pend.pop(0), last=True)
            rec.op("vector", lambda v, m=m, j=j: v.tensor_copy(
                ysb[:, 2 * m * T + j * 512: 2 * m * T + (j + 1) * 512], pool[4][:, :]),
                waits=[(S["tYn"], S["tYn"].v)], inc=ps_rel(4))
            rec.op("vector", lambda v, m=m, j=j: v.tensor_copy(
                ysb[:, (2 * m + 1) * T + j * 512: (2 * m + 1) * T + (j + 1) * 512],
                pool[5][:, :]), inc=ps_rel(5))
            nsp = 2 * (NSTRIP * m + j)
            if nsp >= 2:
                rec.wait("vector", S["dmaQ"], 16 * HQ + 16 * nsp)
            rec.op("vector", lambda v: v.tensor_copy(
                norm_stage[0:1, 0:512], pool[6][0:1, :]), inc=ps_rel(6))
            rec.op("vector", lambda v: v.tensor_copy(
                norm_stage[0:1, 512:1024], pool[7][0:1, :]), inc=ps_rel(7))
            rec.op("vector", lambda v: v.engine_nop(), inc=S["vEv"])
            rec.op("gpsimd", lambda g, m=m, j=j: g.dma_start(
                out=normsp[2 * m:2 * m + 1, j * 512:(j + 1) * 512],
                in_=norm_stage[0:1, 0:512]),
                waits=[(S["vEv"], S["vEv"].v)], inc=S["dmaQ"], incn=16)
            rec.op("gpsimd", lambda g, m=m, j=j: g.dma_start(
                out=normsp[2 * m + 1:2 * m + 2, j * 512:(j + 1) * 512],
                in_=norm_stage[0:1, 512:1024]),
                inc=S["dmaQ"], incn=16)
        rec.op("tensor", lambda t: t.nop(), inc=S["tQf"])

    dump_sbuf("ypre", ysb[:, :], waits=[(S["vEv"], S["vEv"].v)])
    bar()
    a2q.close()
    a2s.close()

    a3b = ExitStack()
    wc_sb = a3b.enter_context(nc.sbuf_tensor("wc_sb", [128, CC * C], BF16))
    bc_sb = a3b.enter_context(nc.sbuf_tensor("bc_sb", [1, C], BF16))
    x_tile = [a3b.enter_context(nc.sbuf_tensor(f"xt{i}", [128, C], F32)) for i in range(2)]
    sq_tile = a3b.enter_context(nc.sbuf_tensor("sq", [128, C], F32))
    stat_sb = a3b.enter_context(nc.sbuf_tensor("stat_sb", [128, 8], F32))

    rec.op("gpsimd", lambda g: g.dma_start(
        out=wc_sb[:, :].rearrange("p (fc c) -> p fc c", c=C),
        in_=Wc.rearrange("(fc p) c -> p fc c", p=128)),
        inc=S["dmaW"], incn=16)
    rec.op("gpsimd", lambda g: g.dma_start(out=bc_sb[:, :], in_=bc_in[:, :]),
           inc=S["dmaW"], incn=16)
    wc_done = S["dmaW"].v

    a3a = ExitStack()
    norm_all = a3a.enter_context(nc.sbuf_tensor("norm_all", [16, T], F32))
    eg16 = a3a.enter_context(nc.sbuf_tensor("eg16", [16, T], F32))
    fac16 = norm_all
    fac16b = a3a.enter_context(nc.sbuf_tensor("fac16b", [16, T], BF16))

    rec.op("sync", lambda sy: sy.dma_start(out=norm_all[:, :], in_=normsp[:, :]),
           waits=[(S["dmaQ"], S["dmaQ"].v)], inc=S["dmaR"], incn=16)
    nall_ld = S["dmaR"].v
    for k in range(4):
        i = k % 2
        ps_free(i)
        rec.op("tensor", lambda t, k=k, i=i: t.matmul(
            pool[i][0:16, :], dup8[:, :], egam[:, k * 512:(k + 1) * 512],
            start=True, stop=True), inc=S["tTr"])
        rec.op("vector", lambda v, k=k, i=i: v.tensor_copy(
            eg16[:, k * 512:(k + 1) * 512], pool[i][0:16, :]),
            waits=[(S["tTr"], S["tTr"].v)], inc=ps_rel(i))
    rec.wait("vector", S["dmaR"], nall_ld)
    rec.wait("vector", prel[0], puse[0])
    rec.wait("vector", prel[1], puse[1])
    rec.op("vector", lambda v: v.tensor_mul(fac16[:, :], norm_all[:, :], eg16[:, :]))
    rec.op("vector", lambda v: v.engine_nop(), inc=S["vFx"])
    rec.wait("vector", S["vFx"], S["vFx"].v)
    rec.op("vector", lambda v: v.tensor_single_scalar(fac16[:, :], fac16[:, :], 1e-6,
                                                      ALU.add))
    rec.op("vector", lambda v: v.reciprocal(fac16[:, :], fac16[:, :]))
    rec.op("vector", lambda v: v.tensor_mul(fac16[:, :], fac16[:, :], eg16[:, :]))
    rec.op("vector", lambda v: v.tensor_copy(fac16b[:, :], fac16[:, :]), inc=S["vFx"])
    fac_done = S["vFx"].v

    for j in range(HQ):
        i0 = (j % 2) * 4
        for i in range(i0, i0 + 4):
            ps_free(i)
        mm512([pool[i0], pool[i0 + 1], pool[i0 + 2], pool[i0 + 3]],
              sel[0:16, j * 128:(j + 1) * 128], fac16b[:, :], start=True, stop=True,
              waits=[(S["vFx"], fac_done), (S["vEv"], sel_ready)], inc=S["tTr"])
        for i in range(4):
            rec.op("vector", lambda v, i=i, i0=i0, j=j: v.tensor_mul(
                ysb[:, j * T + i * 512: j * T + (i + 1) * 512],
                ysb[:, j * T + i * 512: j * T + (i + 1) * 512], pool[i0 + i][:, :]),
                waits=([(S["tTr"], S["tTr"].v)] if i == 0 else ()), inc=ps_rel(i0 + i))
        rec.op("vector", lambda v: v.engine_nop(), inc=S["vFx"])
    ynorm_done = S["vFx"].v
    dump_sbuf("fac16", fac16[:, :], waits=[(S["vFx"], ynorm_done)])
    dump_sbuf("ypost", ysb[:, :], waits=[(S["vFx"], ynorm_done)])
    dump_sbuf("nall", norm_all[:, :], waits=[(S["vFx"], ynorm_done)])
    a3a.close()

    warm(AF.Square)
    warm(AF.Ln)
    warm(AF.Exp)
    out_marks = {}
    for tc in range(TC):
        w = [(S["sLN"], S["sLN"].v)]
        if tc >= 2:
            w.append((S["dmaF"], 32 * (tc - 1)))
            if stage == "A" and (tc - 2) in out_marks:
                w.append((S["dmaO"], out_marks[tc - 2]))
        rec.op("sync", lambda sy, tc=tc: sy.dma_start(out=x_tile[tc % 2][:, :],
                                                      in_=x_in[tc * 128:(tc + 1) * 128, :]),
               waits=w, inc=S["dmaR"], incn=16)
        xre = S["dmaR"].v
        for half in range(2):
            hidx = tc * 2 + half
            i0 = (hidx % 3) * 2
            ps_free(i0); ps_free(i0 + 1)
            for fc in range(CC):
                stat = ysb[:, fc * T + tc * 128: fc * T + (tc + 1) * 128]
                w2 = [(S["vFx"], ynorm_done), (S["dmaW"], wc_done)] if fc == 0 else ()
                mm512([pool[i0], pool[i0 + 1]], stat,
                      wc_sb[:, fc * C + half * 1024: fc * C + (half + 1) * 1024],
                      start=(fc == 0), stop=False, waits=w2)
            rec.op("tensor", lambda t, i0=i0, half=half: t.matmul(
                pool[i0][:, :], ones_row[:, :], bc_sb[:, half * 1024:half * 1024 + 512],
                start=False, stop=True))
            rec.op("tensor", lambda t, i0=i0, half=half: t.matmul(
                pool[i0 + 1][:, :], ones_row[:, :],
                bc_sb[:, half * 1024 + 512:(half + 1) * 1024],
                start=False, stop=True), inc=S["tO"])
            for k in range(2):
                w3 = [(S["tO"], S["tO"].v)]
                if k == 0 and half == 0:
                    w3.append((S["dmaR"], xre))
                sl = slice(half * 1024 + k * 512, half * 1024 + (k + 1) * 512)
                rec.op("vector", lambda v, tc=tc, k=k, i0=i0, sl=sl: v.tensor_add(
                    x_tile[tc % 2][:, sl], pool[i0 + k][:, :], x_tile[tc % 2][:, sl]),
                    waits=w3, inc=ps_rel(i0 + k))
        zt = x_tile[tc % 2]
        rec.op("vector", lambda v, zt=zt: v.reduce_sum(stat_sb[:, 0:1], zt[:, :], axis=AX.X),
               inc=S["vLN"])
        if "z1" in dump:
            if "z1" not in dbg:
                dbg["z1"] = nc.dram_tensor("dbg_z1", [T, C], F32,
                                           kind="ExternalOutput").ap()
            rec.op("sync", lambda sy, tc=tc, zt=zt: sy.dma_start(
                out=dbg["z1"][tc * 128:(tc + 1) * 128, :], in_=zt[:, :]),
                waits=[(S["vLN"], S["vLN"].v)], inc=S["dmaO"], incn=16)
            rec.wait("vector", S["dmaO"], S["dmaO"].v)
        rec.op("scalar", lambda s, zt=zt: s.activation(sq_tile[:, :], zt[:, :], AF.Square),
               waits=[(S["vLN"], S["vLN"].v)], inc=S["sLN"])
        def vf(make):
            rec.op("vector", make, inc=S["vLN"])
            rec.wait("vector", S["vLN"], S["vLN"].v)
        rec.wait("vector", S["sLN"], S["sLN"].v)
        vf(lambda v: v.reduce_sum(stat_sb[:, 1:2], sq_tile[:, :], axis=AX.X))
        vf(lambda v: v.tensor_scalar_mul(stat_sb[:, 2:3], stat_sb[:, 0:1], 1.0 / C))
        vf(lambda v: v.tensor_scalar_mul(stat_sb[:, 3:4], stat_sb[:, 1:2], 1.0 / C))
        vf(lambda v: v.tensor_mul(stat_sb[:, 4:5], stat_sb[:, 2:3], stat_sb[:, 2:3]))
        vf(lambda v: v.tensor_sub(stat_sb[:, 4:5], stat_sb[:, 3:4], stat_sb[:, 4:5]))
        vf(lambda v: v.tensor_single_scalar(stat_sb[:, 4:5], stat_sb[:, 4:5], 1e-5,
                                            ALU.add))
        vf(lambda v: v.reciprocal(stat_sb[:, 5:6], stat_sb[:, 4:5]))
        rec.op("scalar", lambda s: s.activation(stat_sb[:, 6:7], stat_sb[:, 5:6], AF.Ln),
               waits=[(S["vLN"], S["vLN"].v)], inc=S["sLN"])
        rec.wait("scalar", S["sLN"], S["sLN"].v)
        rec.op("scalar", lambda s: s.activation(stat_sb[:, 6:7], stat_sb[:, 6:7], AF.Exp,
                                                scale=gconst[:, 3:4]), inc=S["sLN"])
        rec.wait("vector", S["sLN"], S["sLN"].v)
        vf(lambda v: v.tensor_mul(stat_sb[:, 7:8], stat_sb[:, 2:3], stat_sb[:, 6:7]))
        vf(lambda v: v.tensor_scalar_mul(stat_sb[:, 7:8], stat_sb[:, 7:8], -1.0))
        rec.op("vector", lambda v, zt=zt: v.tensor_scalar(
            zt[:, :], zt[:, :], stat_sb[:, 6:7], stat_sb[:, 7:8], ALU.mult, ALU.add),
            inc=S["sLN"])
        xf_done = S["sLN"].v
        if "st1" in dump:
            if "st1" not in dbg:
                dbg["st1"] = nc.dram_tensor("dbg_st1", [T, 8], F32,
                                            kind="ExternalOutput").ap()
            rec.op("sync", lambda sy, tc=tc: sy.dma_start(
                out=dbg["st1"][tc * 128:(tc + 1) * 128, :], in_=stat_sb[:, :]),
                waits=[(S["sLN"], xf_done)], inc=S["dmaO"], incn=16)
            rec.wait("vector", S["dmaO"], S["dmaO"].v)
        rec.op("sync", lambda sy, tc=tc, zt=zt: sy.dma_start(
            out=xf32[tc * 128:(tc + 1) * 128, :], in_=zt[:, :]),
            waits=[(S["sLN"], xf_done)], inc=S["dmaF"], incn=16)
        rec.op("gpsimd", lambda g, tc=tc, zt=zt: g.dma_start(
            out=xfbf[:, tc * 128:(tc + 1) * 128, :].transpose([1, 0, 2]),
            in_=zt[:, :].rearrange("p (cc f) -> p cc f", f=128)),
            waits=[(S["sLN"], xf_done)], inc=S["dmaF"], incn=16)
        if stage == "A":
            rec.op("sync", lambda sy, tc=tc, zt=zt: sy.dma_start(
                out=out_t[tc * 128:(tc + 1) * 128, :], in_=zt[:, :]),
                waits=[(S["sLN"], xf_done)], inc=S["dmaO"], incn=16)
            out_marks[tc] = S["dmaO"].v
    xfall_done = S["dmaF"].v

    if stage == "A":
        rec.wait("sync", S["dmaO"], S["dmaO"].v)
        rec.flush(blockH)
        a3b.close()
        a_mid.close()
        selctx.close()
        ctx.close()
        return nc

    bar()
    a3b.close()
    a_mid.close()

    b_outer = ExitStack()
    rg_fm = b_outer.enter_context(nc.sbuf_tensor("rg_fm", [128, T], BF16))
    ru_fm = b_outer.enter_context(nc.sbuf_tensor("ru_fm", [128, T], BF16))
    b2s = b_outer.enter_context(nc.sbuf_tensor("b2s", [128, T], BF16))
    sCdt = b_outer.enter_context(nc.sbuf_tensor("sCdt", [128, C], BF16))
    wef = b_outer.enter_context(nc.sbuf_tensor("wef", [32, T], F32))
    wefb = b_outer.enter_context(nc.sbuf_tensor("wefb", [32, T], BF16))
    Rdt = b_outer.enter_context(nc.sbuf_tensor("Rdt", [128, I_FF], BF16))

    b0 = ExitStack()
    xf_fm = b0.enter_context(nc.sbuf_tensor("xf_fm", [128, CC * T], BF16))
    Rgt = b0.enter_context(nc.sbuf_tensor("Rgt", [128, C], BF16))
    Rut = b0.enter_context(nc.sbuf_tensor("Rut", [128, C], BF16))
    gwt = b0.enter_context(nc.sbuf_tensor("gwt", [128, CC * E], BF16))
    sRgt = b0.enter_context(nc.sbuf_tensor("sRgt", [128, C], BF16))
    sRut = b0.enter_context(nc.sbuf_tensor("sRut", [128, C], BF16))
    sRdt = b0.enter_context(nc.sbuf_tensor("sRdt", [128, SI], BF16))
    sUt = b0.enter_context(nc.sbuf_tensor("sUt", [128, 384], BF16))
    sCgt = b0.enter_context(nc.sbuf_tensor("sCgt", [128, SI], BF16))
    sCut = b0.enter_context(nc.sbuf_tensor("sCut", [128, SI], BF16))
    p0_sb = b0.enter_context(nc.sbuf_tensor("p0_sb", [128, 32], F32))
    p1_sb = b0.enter_context(nc.sbuf_tensor("p1_sb", [128, 32], F32))
    tk_sb = b0.enter_context(nc.sbuf_tensor("tk_sb", [128, 32], F32))
    mx_sb = b0.enter_context(nc.sbuf_tensor("mx_sb", [128, 4], F32))
    a1g = b0.enter_context(nc.sbuf_tensor("a1g", [128, 512], BF16))
    a1u = b0.enter_context(nc.sbuf_tensor("a1u", [128, 512], BF16))
    a2g = b0.enter_context(nc.sbuf_tensor("a2g", [128, 512], BF16))
    a2u = b0.enter_context(nc.sbuf_tensor("a2u", [128, 512], BF16))
    gs_sb = [b0.enter_context(nc.sbuf_tensor(f"gs{i}", [128, 512], BF16)) for i in range(3)]
    hs_sb = [b0.enter_context(nc.sbuf_tensor(f"hs{i}", [128, 512], BF16)) for i in range(3)]
    rd_sb = b0.enter_context(nc.sbuf_tensor("rd_sb", [128, 512], BF16))

    def tload(dst, src, extra=()):
        w = [(S["dmaC"], cast_total)] + list(extra)
        rec.op("sync", lambda sy, d=dst, s=src: sy.dma_start(out=d, in_=s, transpose=True),
               waits=w, inc=S["dmaS"], incn=16)

    for cc in range(CC):
        tload(Rgt[:, cc * 128:(cc + 1) * 128], bRg[:, cc * 128:(cc + 1) * 128])
        tload(Rut[:, cc * 128:(cc + 1) * 128], bRu[:, cc * 128:(cc + 1) * 128])
    rr_done = S["dmaS"].v
    for cc in range(CC):
        tload(gwt[:, cc * E:(cc + 1) * E], bgw[:, cc * 128:(cc + 1) * 128])
    gwt_done = S["dmaS"].v
    for cc in range(CC):
        tload(sRgt[:, cc * 128:(cc + 1) * 128], bsRg[:, cc * 128:(cc + 1) * 128])
        tload(sRut[:, cc * 128:(cc + 1) * 128], bsRu[:, cc * 128:(cc + 1) * 128])
    tload(sUt[:, :], bsUUU[:, :])
    tload(sCgt[:, :], bsCg[:, :])
    tload(sCut[:, :], bsCu[:, :])
    sh_done = S["dmaS"].v
    for icb in range(SIC):
        tload(sRdt[:, icb * 128:(icb + 1) * 128], bsRd[:, icb * 128:(icb + 1) * 128])
    tload(sCdt[:, :], bsCd[:, :])
    for icb in range(IC):
        tload(Rdt[:, icb * 128:(icb + 1) * 128], bRd[:, icb * 128:(icb + 1) * 128])
    wt_done = S["dmaS"].v

    xfstg = [b0.enter_context(nc.sbuf_tensor(f"xfs{i}", [128, C], BF16)) for i in range(2)]
    xfs_marks = {}
    for cc in range(CC):
        w = [(S["dmaF"], xfall_done)] if cc == 0 else []
        if cc >= 2:
            w.append((S["tTr"], xfs_marks[cc - 2]))
        rec.op("sync", lambda sy, cc=cc: sy.dma_start(
            out=xfstg[cc % 2][:, :].rearrange("p (tc f) -> p tc f", f=128),
            in_=xfbf[cc, :, :].rearrange("(tc p) f -> p tc f", p=128)),
            waits=w, inc=S["dmaR"], incn=16)
        ld_mark = S["dmaR"].v
        for tc in range(TC):
            i = (cc * TC + tc) % 8
            ps_free(i)
            w2 = [(S["dmaR"], ld_mark)] if tc == 0 else []
            rec.op("tensor", lambda t, tc=tc, cc=cc, i=i: t.matmul(
                pool[i][:, 0:128], xfstg[cc % 2][:, tc * 128:(tc + 1) * 128],
                ident_bf[:, :], start=True, stop=True),
                waits=w2, inc=S["tTr"])
            rec.op("vector", lambda v, tc=tc, cc=cc, i=i: v.tensor_copy(
                xf_fm[:, cc * T + tc * 128: cc * T + (tc + 1) * 128], pool[i][:, 0:128]),
                waits=[(S["tTr"], S["tTr"].v)], inc=ps_rel(i))
        xfs_marks[cc] = S["tTr"].v
    rec.op("vector", lambda v: v.engine_nop(), inc=S["vEv"])
    xffm_ready = S["vEv"].v
    dump_sbuf("xffm", xf_fm[:, :], waits=[(S["vEv"], xffm_ready)])

    for tgt, wsrc in ((rg_fm, Rgt), (ru_fm, Rut)):
        for i in range(4):
            ps_free(i)
        for cc in range(CC):
            stat = wsrc[:, cc * 128:(cc + 1) * 128]
            mm512([pool[0], pool[1], pool[2], pool[3]], stat,
                  xf_fm[:, cc * T:(cc + 1) * T],
                  start=(cc == 0), stop=(cc == CC - 1),
                  waits=([(S["dmaS"], rr_done), (S["vEv"], xffm_ready)]
                         if cc == 0 else ()),
                  inc=(S["tKV"] if cc == CC - 1 else None))
        for i in range(4):
            rec.op("vector", lambda v, tgt=tgt, i=i: v.tensor_copy(
                tgt[:, i * 512:(i + 1) * 512], pool[i][:, :]),
                waits=([(S["tKV"], S["tKV"].v)] if i == 0 else ()), inc=ps_rel(i))

    warm(AF.Exp)
    for tc in range(TC):
        if tc >= 1:
            rec.wait("vector", S["tTr"], S["tTr"].v)
        i = 4 + tc % 2
        ps_free(i)
        for cc in range(CC):
            stat = xf_fm[:, cc * T + tc * 128: cc * T + (tc + 1) * 128]
            rec.op("tensor", lambda t, i=i, stat=stat, cc=cc: t.matmul(
                pool[i][:, 0:32], stat, gwt[:, cc * E:(cc + 1) * E],
                start=(cc == 0), stop=(cc == CC - 1)),
                waits=([(S["dmaS"], gwt_done)] if (tc == 0 and cc == 0) else ()),
                inc=(S["tKV"] if cc == CC - 1 else None))
        tl_done = S["tKV"].v
        rec.op("vector", lambda v, i=i: v.reduce_max(mx_sb[:, 0:1], pool[i][:, 0:32],
                                                     axis=AX.X),
               waits=[(S["tKV"], tl_done)], inc=S["vTk"])
        rec.wait("vector", S["vTk"], S["vTk"].v)
        rec.op("vector", lambda v: v.tensor_scalar_mul(mx_sb[:, 1:2], mx_sb[:, 0:1], -1.0),
               inc=S["vTk"])
        rec.op("scalar", lambda s, i=i: s.activation(p0_sb[:, :], pool[i][:, 0:32], AF.Exp,
                                                     bias=mx_sb[:, 1:2]),
               waits=[(S["vTk"], S["vTk"].v)], inc=S["sFF"])
        pr = ps_rel(i)
        rec.op("vector", lambda v: v.tensor_copy(p1_sb[:, :], p0_sb[:, :]),
               waits=[(S["sFF"], S["sFF"].v)], inc=pr)
        rec.wait("vector", prel[i], puse[i])
        def vfk(make):
            rec.op("vector", make, inc=S["vTk"])
            rec.wait("vector", S["vTk"], S["vTk"].v)
        for r in range(4):
            vfk(lambda v: v.reduce_max(mx_sb[:, 2:3], p1_sb[:, :], axis=AX.X))
            if r < 3:
                vfk(lambda v: v.tensor_single_scalar(
                    tk_sb[:, :], p1_sb[:, :], mx_sb[:, 2:3], ALU.is_equal))
                vfk(lambda v: v.scalar_tensor_tensor(
                    tk_sb[:, :], p1_sb[:, :], 2.0, tk_sb[:, :], ALU.add, ALU.mult))
                vfk(lambda v: v.tensor_sub(p1_sb[:, :], p1_sb[:, :], tk_sb[:, :]))
        vfk(lambda v: v.tensor_single_scalar(tk_sb[:, :], p0_sb[:, :],
                                             mx_sb[:, 2:3], ALU.is_ge))
        vfk(lambda v: v.tensor_mul(tk_sb[:, :], tk_sb[:, :], p0_sb[:, :]))
        vfk(lambda v: v.reduce_sum(mx_sb[:, 3:4], tk_sb[:, :], axis=AX.X))
        vfk(lambda v: v.tensor_single_scalar(mx_sb[:, 3:4], mx_sb[:, 3:4], 1e-20, ALU.add))
        vfk(lambda v: v.reciprocal(mx_sb[:, 3:4], mx_sb[:, 3:4]))
        rec.op("vector", lambda v: v.tensor_single_scalar(tk_sb[:, :], tk_sb[:, :],
                                                          mx_sb[:, 3:4], ALU.mult),
               inc=S["vTk"])
        wtk = S["vTk"].v
        i2 = 6 + tc % 2
        ps_free(i2)
        rec.op("tensor", lambda t, i2=i2: t.matmul(
            pool[i2][0:32, 0:128], tk_sb[:, :], ident[:, :], start=True, stop=True),
            waits=[(S["vTk"], wtk)], inc=S["tTr"])
        rec.op("vector", lambda v, tc=tc, i2=i2: v.tensor_copy(
            wef[:, tc * 128:(tc + 1) * 128], pool[i2][0:32, 0:128]),
            waits=[(S["tTr"], S["tTr"].v)], inc=ps_rel(i2))
    rec.op("vector", lambda v: v.tensor_copy(wefb[:, :], wef[:, :]), inc=S["vTk"])

    rec.op("scalar", lambda s: s.nop(), waits=[(S["vTk"], S["vTk"].v)])
    warm(AF.Silu)

    for ts in range(4):
        for tag, wsrc, dst, i in (("g", sRgt, a1g, 6), ("u", sRut, a1u, 7)):
            ps_free(i)
            for cc in range(CC):
                stat = wsrc[:, cc * 128:(cc + 1) * 128]
                rec.op("tensor", lambda t, i=i, stat=stat, cc=cc, ts=ts: t.matmul(
                    pool[i][:, :], stat,
                    xf_fm[:, cc * T + ts * 512: cc * T + (ts + 1) * 512],
                    start=(cc == 0), stop=(cc == CC - 1)),
                    waits=([(S["dmaS"], sh_done)] if (ts == 0 and cc == 0) else ()),
                    inc=(S["tKV"] if cc == CC - 1 else None))
            rec.op("vector", lambda v, dst=dst, i=i: v.tensor_copy(dst[:, :], pool[i][:, :]),
                   waits=[(S["tKV"], S["tKV"].v)], inc=ps_rel(i))
        rec.op("vector", lambda v: v.engine_nop(), inc=S["vEv"])
        a1_done = S["vEv"].v
        for tag, srcb, dst, ucol, i in (("g", a1g, a2g, 0, 6), ("u", a1u, a2u, 1, 7)):
            ps_free(i)
            rec.op("tensor", lambda t, i=i, srcb=srcb, ucol=ucol: t.matmul(
                pool[i][:, :], sUt[:, ucol * 128:(ucol + 1) * 128], srcb[:, :],
                start=True, stop=True),
                waits=[(S["vEv"], a1_done)], inc=S["tKV"])
            rec.op("vector", lambda v, dst=dst, i=i: v.tensor_copy(dst[:, :], pool[i][:, :]),
                   waits=[(S["tKV"], S["tKV"].v)], inc=ps_rel(i))
        rec.op("vector", lambda v: v.engine_nop(), inc=S["vEv"])
        a2_done = S["vEv"].v
        ps_free(6)
        h_done = {}

        def rd_mm_s(jj, ts=None):
            wl = [(S["vEv"], h_done[jj])]
            if jj == 0:
                wl.append((S["dmaS"], wt_done))
            rec.op("tensor", lambda t, jj=jj: t.matmul(
                pool[6][:, :], sRdt[:, jj * 128:(jj + 1) * 128], hs_sb[jj % 3][:, :],
                start=(jj == 0), stop=(jj == SIC - 1)),
                waits=wl,
                inc=(S["tKV"] if jj == SIC - 1 else None))

        for icb in range(SIC):
            ig = icb % 3
            iu = 3 + icb % 3
            ps_free(ig); ps_free(iu)
            rec.op("tensor", lambda t, ig=ig, icb=icb: t.matmul(
                pool[ig][:, :], sCgt[:, icb * 128:(icb + 1) * 128], a2g[:, :],
                start=True, stop=True),
                waits=([(S["vEv"], a2_done)] if icb == 0 else ()), inc=S["tKV"])
            gdone = S["tKV"].v
            rec.op("tensor", lambda t, iu=iu, icb=icb: t.matmul(
                pool[iu][:, :], sCut[:, icb * 128:(icb + 1) * 128], a2u[:, :],
                start=True, stop=True), inc=S["tKV"])
            udone = S["tKV"].v
            rec.op("scalar", lambda s, ig=ig, icb=icb: s.activation(
                gs_sb[icb % 3][:, :], pool[ig][:, :], AF.Silu),
                waits=[(S["tKV"], gdone)], inc=ps_rel(ig))
            rec.op("scalar", lambda s: s.nop(), inc=S["sFF"])
            rec.op("vector", lambda v, iu=iu, icb=icb: v.tensor_mul(
                hs_sb[icb % 3][:, :], gs_sb[icb % 3][:, :], pool[iu][:, :]),
                waits=[(S["sFF"], S["sFF"].v), (S["tKV"], udone)], inc=ps_rel(iu))
            rec.op("vector", lambda v: v.engine_nop(), inc=S["vEv"])
            h_done[icb] = S["vEv"].v
            if icb >= 2:
                rd_mm_s(icb - 2)
        rd_mm_s(SIC - 2)
        rd_mm_s(SIC - 1)
        rec.op("vector", lambda v: v.tensor_copy(rd_sb[:, :], pool[6][:, :]),
               waits=[(S["tKV"], S["tKV"].v)], inc=ps_rel(6))
        rec.op("vector", lambda v: v.engine_nop(), inc=S["vEv"])
        ps_free(7)
        rec.op("tensor", lambda t: t.matmul(pool[7][:, :], sUt[:, 256:384], rd_sb[:, :],
                                            start=True, stop=True),
               waits=[(S["vEv"], S["vEv"].v)], inc=S["tKV"])
        rec.op("vector", lambda v, ts=ts: v.tensor_copy(
            b2s[:, ts * 512:(ts + 1) * 512], pool[7][:, :]),
            waits=[(S["tKV"], S["tKV"].v)], inc=ps_rel(7))

    dump_sbuf("rg", rg_fm[:, :], waits=[(S["vEv"], S["vEv"].v), (S["vTk"], S["vTk"].v)])
    dump_sbuf("ru", ru_fm[:, :], waits=[(S["vEv"], S["vEv"].v)])
    dump_sbuf("wef", wef[:, :], waits=[(S["vTk"], S["vTk"].v)])
    dump_sbuf("b2s", b2s[:, :], waits=[(S["vEv"], S["vEv"].v)])
    b0.close()
    bar()

    b1 = ExitStack()
    eUUU = [b1.enter_context(nc.sbuf_tensor(f"eU{i}", [128, 384], BF16)) for i in range(2)]
    eCg = [b1.enter_context(nc.sbuf_tensor(f"eCg{i}", [128, I_FF], BF16)) for i in range(2)]
    eCu = [b1.enter_context(nc.sbuf_tensor(f"eCu{i}", [128, I_FF], BF16)) for i in range(2)]
    eCd = [b1.enter_context(nc.sbuf_tensor(f"eCd{i}", [128, 512], BF16)) for i in range(4)]
    pg_sb = b1.enter_context(nc.sbuf_tensor("pg_sb", [128, 512], BF16))
    pu_sb = b1.enter_context(nc.sbuf_tensor("pu_sb", [128, 512], BF16))
    web_sb = b1.enter_context(nc.sbuf_tensor("web_sb", [128, 512], F32))
    pd_all = b1.enter_context(nc.sbuf_tensor("pd_all", [128, E * 512], BF16))
    ge_sb = [b1.enter_context(nc.sbuf_tensor(f"ge{i}", [128, 512], BF16)) for i in range(3)]
    he_sb = [b1.enter_context(nc.sbuf_tensor(f"he{i}", [128, 512], BF16)) for i in range(3)]
    rde_sb = b1.enter_context(nc.sbuf_tensor("rde_sb", [128, 512], BF16))
    z2_sb = [b1.enter_context(nc.sbuf_tensor(f"z2_{i}", [128, C], F32)) for i in range(4)]
    xfb_sb = [b1.enter_context(nc.sbuf_tensor(f"xfb{i}", [128, C], F32)) for i in range(4)]
    o_sb = [b1.enter_context(nc.sbuf_tensor(f"osb{i}", [128, C], F32)) for i in range(2)]
    sq2_sb = b1.enter_context(nc.sbuf_tensor("sq2", [128, C], F32))
    st2_sb = b1.enter_context(nc.sbuf_tensor("st2", [128, 8], F32))

    n_cd = 0
    for blk in range(NBLK):
        tb = blk * 512
        warm(AF.Silu)
        rec.wait("vector", S["tO"], S["tO"].v)
        for e in range(E):
            eb = e % 2
            ke = blk * E + e
            wprev = ([(S["tWf"], S["tWf"].v - 1)] if ke >= 2 else [])
            if blk == 0:
                wprev = wprev + [(S["dmaC"], cast_total)]
                if e >= 2:
                    wprev = wprev + [(S["bSt"], 48 * (e - 1))]
                rec.op("sync", lambda sy, e=e, eb=eb: sy.dma_start(
                    out=eUUU[eb][:, :], in_=bUUU[e, :, :], transpose=True),
                    waits=wprev, inc=S["dmaS"], incn=16)
                rec.op("sync", lambda sy, e=e, eb=eb: sy.dma_start(
                    out=eCg[eb][:, :], in_=bCg[e, :, :], transpose=True),
                    inc=S["dmaS"], incn=16)
                rec.op("sync", lambda sy, e=e, eb=eb: sy.dma_start(
                    out=eCu[eb][:, :], in_=bCu[e, :, :], transpose=True),
                    inc=S["dmaS"], incn=16)
                ew_done = S["dmaS"].v
                rec.op("gpsimd", lambda g, e=e, eb=eb: g.dma_start(
                    out=bUUUT[e, :, :], in_=eUUU[eb][:, :]),
                    waits=[(S["dmaS"], ew_done)], inc=S["bSt"], incn=16)
                rec.op("gpsimd", lambda g, e=e, eb=eb: g.dma_start(
                    out=bCgT[e, :, :], in_=eCg[eb][:, :]), inc=S["bSt"], incn=16)
                rec.op("gpsimd", lambda g, e=e, eb=eb: g.dma_start(
                    out=bCuT[e, :, :], in_=eCu[eb][:, :]), inc=S["bSt"], incn=16)
            else:
                rec.op("sync", lambda sy, e=e, eb=eb: sy.dma_start(
                    out=eUUU[eb][:, :], in_=bUUUT[e, :, :]),
                    waits=wprev + [(S["bSt"], 48 * E)], inc=S["dmaS"], incn=16)
                rec.op("sync", lambda sy, e=e, eb=eb: sy.dma_start(
                    out=eCg[eb][:, :], in_=bCgT[e, :, :]), inc=S["dmaS"], incn=16)
                rec.op("sync", lambda sy, e=e, eb=eb: sy.dma_start(
                    out=eCu[eb][:, :], in_=bCuT[e, :, :]), inc=S["dmaS"], incn=16)
                ew_done = S["dmaS"].v
            ps_free(6)
            rec.op("tensor", lambda t, eb=eb, tb=tb: t.matmul(
                pool[6][:, :], eUUU[eb][:, 0:128], rg_fm[:, tb:tb + 512],
                start=True, stop=True),
                waits=[(S["dmaS"], ew_done)], inc=S["tKV"])
            rec.op("vector", lambda v: v.tensor_copy(pg_sb[:, :], pool[6][:, :]),
                   waits=[(S["tKV"], S["tKV"].v)], inc=ps_rel(6))
            ps_free(7)
            rec.op("tensor", lambda t, eb=eb, tb=tb: t.matmul(
                pool[7][:, :], eUUU[eb][:, 128:256], ru_fm[:, tb:tb + 512],
                start=True, stop=True), inc=S["tKV"])
            rec.op("vector", lambda v: v.tensor_copy(pu_sb[:, :], pool[7][:, :]),
                   waits=[(S["tKV"], S["tKV"].v)], inc=ps_rel(7))
            ps_free(6)
            rec.op("tensor", lambda t, e=e, tb=tb: t.matmul(
                pool[6][:, :], sel[:, e * 128:(e + 1) * 128], wefb[:, tb:tb + 512],
                start=True, stop=True),
                waits=[(S["vTk"], S["vTk"].v)], inc=S["tKV"])
            rec.op("vector", lambda v: v.tensor_copy(web_sb[:, :], pool[6][:, :]),
                   waits=[(S["tKV"], S["tKV"].v)], inc=ps_rel(6))
            rec.op("vector", lambda v: v.engine_nop(), inc=S["vEv"])
            pgu_done = S["vEv"].v
            ps_free(6)
            h_done_e = {}

            def rd_mm_e(jj, _f=(blk == 0 and e == 0)):
                wl = [(S["vEv"], h_done_e[jj])]
                if jj == 0 and _f:
                    wl.append((S["dmaS"], wt_done))
                rec.op("tensor", lambda t, jj=jj: t.matmul(
                    pool[6][:, :], Rdt[:, jj * 128:(jj + 1) * 128], he_sb[jj % 3][:, :],
                    start=(jj == 0), stop=(jj == IC - 1)),
                    waits=wl,
                    inc=(S["tKV"] if jj == IC - 1 else None))

            for icb in range(IC):
                ig = icb % 3
                iu = 3 + icb % 3
                ps_free(ig); ps_free(iu)
                rec.op("tensor", lambda t, eb=eb, icb=icb, ig=ig: t.matmul(
                    pool[ig][:, :], eCg[eb][:, icb * 128:(icb + 1) * 128], pg_sb[:, :],
                    start=True, stop=True),
                    waits=([(S["vEv"], pgu_done)] if icb == 0 else ()), inc=S["tKV"])
                gdone = S["tKV"].v
                rec.op("tensor", lambda t, eb=eb, icb=icb, iu=iu: t.matmul(
                    pool[iu][:, :], eCu[eb][:, icb * 128:(icb + 1) * 128], pu_sb[:, :],
                    start=True, stop=True), inc=S["tKV"])
                udone = S["tKV"].v
                rec.op("scalar", lambda s, icb=icb, ig=ig: s.activation(
                    ge_sb[icb % 3][:, :], pool[ig][:, :], AF.Silu),
                    waits=[(S["tKV"], gdone)], inc=ps_rel(ig))
                rec.op("scalar", lambda s: s.nop(), inc=S["sFF"])
                rec.op("vector", lambda v, icb=icb, iu=iu: v.tensor_mul(
                    he_sb[icb % 3][:, :], ge_sb[icb % 3][:, :], pool[iu][:, :]),
                    waits=[(S["sFF"], S["sFF"].v), (S["tKV"], udone)], inc=ps_rel(iu))
                rec.op("vector", lambda v: v.engine_nop(), inc=S["vEv"])
                h_done_e[icb] = S["vEv"].v
                if icb >= 2:
                    rd_mm_e(icb - 2)
            rd_mm_e(IC - 2)
            rd_mm_e(IC - 1)
            rec.op("vector", lambda v: v.tensor_copy(rde_sb[:, :], pool[6][:, :]),
                   waits=[(S["tKV"], S["tKV"].v)], inc=ps_rel(6))
            rec.op("vector", lambda v: v.engine_nop(), inc=S["vEv"])
            ps_free(7)
            rec.op("tensor", lambda t, eb=eb: t.matmul(
                pool[7][:, :], eUUU[eb][:, 256:384], rde_sb[:, :], start=True, stop=True),
                waits=[(S["vEv"], S["vEv"].v)], inc=S["tWf"])
            rec.op("vector", lambda v, e=e: v.tensor_mul(
                pd_all[:, e * 512:(e + 1) * 512], pool[7][:, :], web_sb[:, :]),
                waits=[(S["tWf"], S["tWf"].v)], inc=ps_rel(7))
            rec.op("vector", lambda v: v.engine_nop(), inc=S["vPd"])
        pd_done = S["vPd"].v
        if "pd" in dump:
            if "pd" not in dbg:
                dbg["pd"] = nc.dram_tensor("dbg_pd", [NBLK, 128, E * 512], BF16,
                                           kind="ExternalOutput").ap()
            rec.op("sync", lambda sy, blk=blk: sy.dma_start(
                out=dbg["pd"][blk, :, :], in_=pd_all[:, :]),
                waits=[(S["vPd"], pd_done)], inc=S["dmaO"], incn=16)

        for tcl in range(4):
            tcg = blk * 4 + tcl
            w = []
            if blk > 0 and tcl == 0:
                w.append((S["sLN"], S["sLN"].v))
            rec.op("sync", lambda sy, tcg=tcg, tcl=tcl: sy.dma_start(
                out=xfb_sb[tcl][:, :], in_=xf32[tcg * 128:(tcg + 1) * 128, :]),
                waits=w, inc=S["dmaR"], incn=16)
        xfb_done = S["dmaR"].v
        for hq in range(4):
            tiles = [4 * (hq % 2) + i for i in range(4)]
            for e in range(E + 1):
                if e < E:
                    eb = n_cd % 4
                    n_cd += 1
                    wprev = ([(S["tCd"], n_cd - 4)] if n_cd >= 5 else [])
                    if blk == 0:
                        if n_cd >= 5:
                            wprev = wprev + [(S["bTl"], 16 * (n_cd - 4))]
                        rec.op("sync", lambda sy, e=e, eb=eb, hq=hq: sy.dma_start(
                            out=eCd[eb][:, :], in_=bCd[e, hq * 512:(hq + 1) * 512, :],
                            transpose=True),
                            waits=wprev, inc=S["dmaS"], incn=16)
                        cd_done = S["dmaS"].v
                        rec.op("gpsimd", lambda g, e=e, eb=eb, hq=hq: g.dma_start(
                            out=bCdT[e, hq, :, :], in_=eCd[eb][:, :]),
                            waits=[(S["dmaS"], cd_done)], inc=S["bTl"], incn=16)
                    else:
                        rec.op("sync", lambda sy, e=e, eb=eb, hq=hq: sy.dma_start(
                            out=eCd[eb][:, :], in_=bCdT[e, hq, :, :]),
                            waits=wprev + [(S["bTl"], 16 * 128)],
                            inc=S["dmaS"], incn=16)
                        cd_done = S["dmaS"].v
                for tcl in range(4):
                    i = tiles[tcl]
                    if e == 0:
                        ps_free(i)
                    if e < E:
                        stat = pd_all[:, e * 512 + tcl * 128: e * 512 + (tcl + 1) * 128]
                        mov = eCd[eb][:, :]
                        w4 = ([(S["dmaS"], cd_done), (S["vPd"], pd_done)] if tcl == 0 else ())
                    else:
                        stat = b2s[:, tb + tcl * 128: tb + (tcl + 1) * 128]
                        mov = sCdt[:, hq * 512:(hq + 1) * 512]
                        w4 = ()
                    inc = None
                    if e < E and tcl == 3:
                        inc = S["tCd"]
                    elif e == E and tcl == 3:
                        inc = S["tO"]
                    rec.op("tensor", lambda t, i=i, stat=stat, mov=mov, e=e: t.matmul(
                        pool[i][:, :], stat, mov, start=(e == 0), stop=(e == E)),
                        waits=w4, inc=inc)
            tO_now = S["tO"].v
            for tcl in range(4):
                i = tiles[tcl]
                w5 = [(S["tO"], tO_now)]
                if hq == 0 and tcl == 0:
                    w5.append((S["dmaR"], xfb_done))
                    w5.append((S["sLN"], S["sLN"].v))
                rec.op("vector", lambda v, tcl=tcl, hq=hq, i=i: v.tensor_add(
                    z2_sb[tcl][:, hq * 512:(hq + 1) * 512], pool[i][:, :],
                    xfb_sb[tcl][:, hq * 512:(hq + 1) * 512]),
                    waits=w5, inc=ps_rel(i))
        rec.op("vector", lambda v: v.engine_nop(), inc=S["vZ2"])
        z2_done = S["vZ2"].v
        if "z2" in dump:
            if "z2" not in dbg:
                dbg["z2"] = nc.dram_tensor("dbg_z2", [T, C], F32,
                                           kind="ExternalOutput").ap()
            for tcl in range(4):
                tcg = blk * 4 + tcl
                rec.op("sync", lambda sy, tcg=tcg, tcl=tcl: sy.dma_start(
                    out=dbg["z2"][tcg * 128:(tcg + 1) * 128, :], in_=z2_sb[tcl][:, :]),
                    waits=[(S["vZ2"], z2_done)], inc=S["dmaO"], incn=16)
            rec.wait("vector", S["dmaO"], S["dmaO"].v)

        warm(AF.Square)
        warm(AF.Ln)
        warm(AF.Exp)
        for tcl in range(4):
            tcg = blk * 4 + tcl
            rec.op("vector", lambda v, tcl=tcl: v.reduce_sum(
                st2_sb[:, 0:1], z2_sb[tcl][:, :], axis=AX.X),
                waits=[(S["vZ2"], z2_done)], inc=S["vLN"])
            rec.op("scalar", lambda s, tcl=tcl: s.activation(
                sq2_sb[:, :], z2_sb[tcl][:, :], AF.Square),
                waits=[(S["vLN"], S["vLN"].v)], inc=S["sLN"])
            def vf(make):
                rec.op("vector", make, inc=S["vLN"])
                rec.wait("vector", S["vLN"], S["vLN"].v)
            rec.wait("vector", S["sLN"], S["sLN"].v)
            vf(lambda v: v.reduce_sum(st2_sb[:, 1:2], sq2_sb[:, :], axis=AX.X))
            vf(lambda v: v.tensor_scalar_mul(st2_sb[:, 2:3], st2_sb[:, 0:1], 1.0 / C))
            vf(lambda v: v.tensor_scalar_mul(st2_sb[:, 3:4], st2_sb[:, 1:2], 1.0 / C))
            vf(lambda v: v.tensor_mul(st2_sb[:, 4:5], st2_sb[:, 2:3], st2_sb[:, 2:3]))
            vf(lambda v: v.tensor_sub(st2_sb[:, 4:5], st2_sb[:, 3:4], st2_sb[:, 4:5]))
            vf(lambda v: v.tensor_single_scalar(st2_sb[:, 4:5], st2_sb[:, 4:5], 1e-5,
                                                ALU.add))
            vf(lambda v: v.reciprocal(st2_sb[:, 5:6], st2_sb[:, 4:5]))
            rec.op("scalar", lambda s: s.activation(st2_sb[:, 6:7], st2_sb[:, 5:6], AF.Ln),
                   waits=[(S["vLN"], S["vLN"].v)], inc=S["sLN"])
            rec.wait("scalar", S["sLN"], S["sLN"].v)
            rec.op("scalar", lambda s: s.activation(st2_sb[:, 6:7], st2_sb[:, 6:7], AF.Exp,
                                                    scale=gconst[:, 3:4]), inc=S["sLN"])
            rec.wait("vector", S["sLN"], S["sLN"].v)
            vf(lambda v: v.tensor_mul(st2_sb[:, 7:8], st2_sb[:, 2:3], st2_sb[:, 6:7]))
            vf(lambda v: v.tensor_scalar_mul(st2_sb[:, 7:8], st2_sb[:, 7:8], -1.0))
            ob = tcg % 2
            w6 = []
            if tcg >= 2:
                w6.append((S["dmaO"], 16 * (tcg - 1)))
            rec.op("vector", lambda v, tcl=tcl, ob=ob: v.tensor_scalar(
                o_sb[ob][:, :], z2_sb[tcl][:, :], st2_sb[:, 6:7], st2_sb[:, 7:8],
                ALU.mult, ALU.add),
                waits=w6, inc=S["sLN"])
            rec.op("sync", lambda sy, tcg=tcg, ob=ob: sy.dma_start(
                out=out_t[tcg * 128:(tcg + 1) * 128, :], in_=o_sb[ob][:, :]),
                waits=[(S["sLN"], S["sLN"].v)], inc=S["dmaO"], incn=16)

    rec.wait("sync", S["dmaO"], S["dmaO"].v)
    rec.flush(blockH)
    b1.close()
    b_outer.close()
    selctx.close()
    p0ctx.close()
    ctx.close()
    return nc


_PROG_CACHE = {}


def _get_prog(stage="full"):
    if stage not in _PROG_CACHE:
        _PROG_CACHE[stage] = build_core_program(stage)
    return _PROG_CACHE[stage]


def kernel(**inputs):
    nc = _get_prog("full")
    B = 8
    x = inputs["L0_attn_hidden_states"]
    base = {
        "Wq": inputs["L0_attn_Wq"], "Wk": inputs["L0_attn_Wk"], "Wv": inputs["L0_attn_Wv"],
        "Wg": inputs["L0_attn_Wg"], "Wc": inputs["L0_attn_Wc"],
        "bc": np.ascontiguousarray(np.asarray(inputs["L0_attn_bc"]).reshape(1, C)),
        "gw": inputs["L0_ffn_gate_w"],
        "Rg": inputs["L0_ffn_Rg"], "Ru": inputs["L0_ffn_Ru"], "Rd": inputs["L0_ffn_Rd"],
        "Ug": inputs["L0_ffn_Ug"], "Uu": inputs["L0_ffn_Uu"], "Ud": inputs["L0_ffn_Ud"],
        "Cg": inputs["L0_ffn_Cg"], "Cu": inputs["L0_ffn_Cu"], "Cd": inputs["L0_ffn_Cd"],
        "sRg": inputs["L0_ffn_s_Rg"], "sRu": inputs["L0_ffn_s_Ru"],
        "sRd": inputs["L0_ffn_s_Rd"],
        "sUg": inputs["L0_ffn_s_Ug"], "sUu": inputs["L0_ffn_s_Uu"],
        "sUd": inputs["L0_ffn_s_Ud"],
        "sCg": inputs["L0_ffn_s_Cg"], "sCu": inputs["L0_ffn_s_Cu"],
        "sCd": inputs["L0_ffn_s_Cd"],
    }
    base = {k: np.ascontiguousarray(np.asarray(v), dtype=np.float32)
            for k, v in base.items()}
    in_maps = []
    for c in range(B):
        m = dict(base)
        m["x"] = np.ascontiguousarray(np.asarray(x)[c], dtype=np.float32)
        in_maps.append(m)
    res = run_bass_kernel_spmd(nc, in_maps, core_ids=list(range(B)))
    return np.stack([np.asarray(r["out"]) for r in res.results], axis=0)


# version 12
# speedup vs baseline: 1.0188x; 1.0188x over previous
import numpy as np
from contextlib import ExitStack

import concourse.bass as bass
import concourse.mybir as mybir
from concourse.bass_utils import run_bass_kernel_spmd

F32 = mybir.dt.float32
BF16 = mybir.dt.bfloat16
AF = mybir.ActivationFunctionType
ALU = mybir.AluOpType
AX = mybir.AxisListType

T = 2048
C = 2048
HKV = 8
HQ = 16
D = 128
KVD = 1024
TC = 16
CC = 16
NSTRIP = 4
GATE_OFFSET = 6.906768
LN128 = float(np.log(128.0))

E = 32
I_FF = 1408
IC = 11
SI = 2816
SIC = 22
RK = 128
NBLK = 4

ENGINES = ("tensor", "vector", "scalar", "sync", "gpsimd")


class Sem:
    __slots__ = ("h", "v", "name")

    def __init__(self, nc, ctx, name):
        self.h = ctx.enter_context(nc.semaphore(name))
        self.v = 0
        self.name = name


class Rec:
    def __init__(self):
        self.q = {e: [] for e in ENGINES}

    def op(self, eng, make, waits=(), inc=None, incn=1):
        wl = [(s.h, v) for (s, v) in waits]

        def run(e, make=make, wl=wl, inc=inc, incn=incn):
            for h, v in wl:
                e.wait_ge(h, v)
            ins = make(e)
            if inc is not None:
                ins.then_inc(inc.h, incn)
        self.q[eng].append(run)
        if inc is not None:
            inc.v += incn

    def wait(self, eng, sem, v=None):
        vv = sem.v if v is None else v
        self.q[eng].append(lambda e, h=sem.h, vv=vv: e.wait_ge(h, vv))

    def flush(self, block):
        for name in ENGINES:
            fns = self.q[name]
            if fns:
                getattr(block, name)(lambda eng, fns=list(fns): [f(eng) for f in fns])
        self.q = {e: [] for e in ENGINES}


def build_core_program(stage="full", dump=()):
    nc = bass.Bass("TRN2", target_bir_lowering=False)
    ctx = ExitStack()

    def din(name, shape):
        return nc.dram_tensor(name, list(shape), F32, kind="ExternalInput").ap()

    x_in = din("x", (T, C))
    Wq = din("Wq", (C, C)); Wk = din("Wk", (C, KVD)); Wv = din("Wv", (C, KVD))
    Wg = din("Wg", (C, HKV)); Wc = din("Wc", (C, C)); bc_in = din("bc", (1, C))
    gw = din("gw", (E, C))
    Rg = din("Rg", (RK, C)); Ru = din("Ru", (RK, C)); Rd = din("Rd", (RK, I_FF))
    Ug = din("Ug", (E, RK, RK)); Uu = din("Uu", (E, RK, RK)); Ud = din("Ud", (E, RK, RK))
    Cg = din("Cg", (E, I_FF, RK)); Cu = din("Cu", (E, I_FF, RK)); Cd = din("Cd", (E, C, RK))
    sRg = din("sRg", (RK, C)); sRu = din("sRu", (RK, C)); sRd = din("sRd", (RK, SI))
    sUg = din("sUg", (RK, RK)); sUu = din("sUu", (RK, RK)); sUd = din("sUd", (RK, RK))
    sCg = din("sCg", (SI, RK)); sCu = din("sCu", (SI, RK)); sCd = din("sCd", (C, RK))

    out_t = nc.dram_tensor("out", [T, C], F32, kind="ExternalOutput").ap()

    def scratch(name, shape, dtype=BF16):
        kind = "ExternalOutput" if name in dump else "Internal"
        return nc.dram_tensor(name, list(shape), dtype, kind=kind).ap()
    dbg = {}
    def dump_sbuf(tag, sb_ap, waits=()):
        if tag not in dump:
            return
        shp = list(sb_ap.shape)
        dbg[tag] = nc.dram_tensor("dbg_" + tag, shp, sb_ap.dtype,
                                  kind="ExternalOutput").ap()
        rec.op("sync", lambda sy, d=dbg[tag], s=sb_ap: sy.dma_start(out=d[:, :], in_=s),
               waits=waits, inc=S["dmaO"], incn=16)
    xbf = scratch("xbf", (CC, T, 128))
    qsp = scratch("qsp", (HQ, D, T))
    xf32 = scratch("xf32", (T, C), F32)
    xfbf = scratch("xfbf", (CC, T, 128))
    bUUU = scratch("bUUU", (E, 3 * RK, RK))
    bCg = scratch("bCg", (E, I_FF, RK))
    bCu = scratch("bCu", (E, I_FF, RK))
    bCd = scratch("bCd", (E, C, RK))
    bRg = scratch("bRg", (RK, C)); bRu = scratch("bRu", (RK, C)); bRd = scratch("bRd", (RK, I_FF))
    bgw = scratch("bgw", (E, C))
    bsRg = scratch("bsRg", (RK, C)); bsRu = scratch("bsRu", (RK, C)); bsRd = scratch("bsRd", (RK, SI))
    bsUUU = scratch("bsUUU", (3 * RK, RK))
    bsCg = scratch("bsCg", (SI, RK)); bsCu = scratch("bsCu", (SI, RK)); bsCd = scratch("bsCd", (C, RK))
    normsp = scratch("normsp", (16, T), F32)
    bUUUT = scratch("bUUUT", (E, 128, 384))
    bCgT = scratch("bCgT", (E, 128, I_FF))
    bCuT = scratch("bCuT", (E, 128, I_FF))
    bCdT = scratch("bCdT", (E, 4, 128, 512))

    ident = ctx.enter_context(nc.sbuf_tensor("ident", [128, 128], F32, side="right"))
    band = ctx.enter_context(nc.sbuf_tensor("band", [128, 896], BF16, side="right"))
    ones_col = ctx.enter_context(nc.sbuf_tensor("ones_col", [128, 1], BF16, side="right"))
    ones_row = ctx.enter_context(nc.sbuf_tensor("ones_row", [1, 128], BF16, side="right"))
    a2h_tm = ctx.enter_context(nc.sbuf_tensor("a2h_tm", [128, TC * HKV], F32, side="right"))
    a2q_tm = ctx.enter_context(nc.sbuf_tensor("a2q_tm", [128, TC * HKV], F32, side="right"))
    egam = ctx.enter_context(nc.sbuf_tensor("egam", [8, T], F32, side="right"))
    gconst = ctx.enter_context(nc.sbuf_tensor("gconst", [128, 6], F32, side="right"))
    dup8 = ctx.enter_context(nc.sbuf_tensor("dup8", [8, 16], F32, side="right"))
    ident_bf = ctx.enter_context(nc.sbuf_tensor("ident_bf", [128, 128], BF16, side="right"))

    bstg = [ctx.enter_context(nc.sbuf_tensor(f"bstg{i}", [128, C], BF16, side="right"))
            for i in range(2)]
    pool = [None] * 8
    for i in (0, 1, 4, 5):
        pool[i] = ctx.enter_context(nc.psum_tensor(f"pp{i}", [128, 512], F32))
    pool67A = ExitStack()
    for i in (2, 3, 6, 7):
        pool[i] = pool67A.enter_context(nc.psum_tensor(f"pp{i}a", [128, 512], F32))

    S = {}
    for n in ("bar", "dmaC", "dmaX", "dmaW", "dmaQ", "dmaS", "dmaF", "dmaO", "dmaR",
              "tKV", "tG", "tQ", "tS", "tYn", "tQf", "tWf", "tO", "tTr", "tCd",
              "sAe", "sLN", "sFF", "bTl", "bSt",
              "vEv", "vAm", "vAe", "vLN", "vPd", "vTk", "vFx", "vZ2"):
        S[n] = Sem(nc, ctx, n)
    prel = [Sem(nc, ctx, f"prel{i}") for i in range(8)]
    puse = [0] * 8

    rec = Rec()
    blockH = ctx.enter_context(nc.Block())

    def ps_free(i, eng="tensor"):
        rec.wait(eng, prel[i], puse[i])

    def ps_rel(i):
        puse[i] += 1
        return prel[i]

    def bar():
        base = S["bar"].v
        for eng in ENGINES:
            rec.op(eng, lambda e: e.nop(), inc=S["bar"])
        for eng in ENGINES:
            rec.wait(eng, S["bar"], base + 5)

    warm_out = ctx.enter_context(nc.sbuf_tensor("warm_out", [1, 4], F32, side="right"))

    def warm(func, n=2):
        for _ in range(n):
            rec.op("scalar", lambda s, func=func: s.activation(
                warm_out[0:1, 0:1], gconst[0:1, 0:1], func))

    def mm512(ps_list, stat, mov, start, stop, waits=(), inc=None, rows=128):
        n = len(ps_list)
        for k in range(n):
            rec.op("tensor",
                   lambda t, o=ps_list[k], s=stat, m=mov[:, 512 * k:512 * (k + 1)],
                          st=start, sp=stop, rows=rows: t.matmul(
                       o[0:rows, :], s, m, start=st, stop=sp),
                   waits=(waits if k == 0 else ()),
                   inc=(inc if k == n - 1 else None))

    p0ctx = ExitStack()
    iota_f = p0ctx.enter_context(nc.sbuf_tensor("iota_f", [128, 896], F32))
    rec.op("gpsimd", lambda g: g.iota(iota_f[:, :], pattern=[[1, 896]], base=-384,
                                      channel_multiplier=-1,
                                      allow_small_or_imprecise_dtypes=True), inc=S["dmaC"])
    rec.op("vector", lambda v: v.tensor_single_scalar(band[:, :], iota_f[:, :], 0.0,
                                                      ALU.is_ge),
           waits=[(S["dmaC"], S["dmaC"].v)], inc=S["vEv"])
    rec.op("gpsimd", lambda g: g.iota(iota_f[:, 0:128], pattern=[[1, 128]], base=0,
                                      channel_multiplier=-1,
                                      allow_small_or_imprecise_dtypes=True),
           waits=[(S["vEv"], S["vEv"].v)], inc=S["dmaC"])
    rec.op("vector", lambda v: v.tensor_single_scalar(ident[:, :], iota_f[:, 0:128], 0.0,
                                                      ALU.is_equal),
           waits=[(S["dmaC"], S["dmaC"].v)], inc=S["vEv"])
    rec.wait("vector", S["vEv"], S["vEv"].v)
    rec.op("vector", lambda v: v.tensor_copy(ident_bf[:, :], ident[:, :]), inc=S["vEv"])
    rec.wait("vector", S["vEv"], S["vEv"].v)
    rec.op("gpsimd", lambda g: g.iota(dup8[:, :], pattern=[[-1, 8], [0, 2]],
                                      base=0, channel_multiplier=1,
                                      allow_small_or_imprecise_dtypes=True),
           inc=S["dmaC"])
    rec.op("vector", lambda v: v.tensor_single_scalar(dup8[:, :], dup8[:, :], 0.0,
                                                      ALU.is_equal),
           waits=[(S["dmaC"], S["dmaC"].v)], inc=S["vEv"])
    rec.op("vector", lambda v: v.memset(ones_col[:, :], 1.0), inc=S["vEv"])
    rec.op("vector", lambda v: v.memset(ones_row[:, :], 1.0), inc=S["vEv"])
    rec.op("vector", lambda v: v.memset(gconst[:, 0:1], GATE_OFFSET), inc=S["vEv"])
    rec.op("vector", lambda v: v.memset(gconst[:, 1:2], -0.5), inc=S["vEv"])
    rec.op("vector", lambda v: v.memset(gconst[:, 2:3], -0.5 * LN128), inc=S["vEv"])
    rec.op("vector", lambda v: v.memset(gconst[:, 3:4], 0.5), inc=S["vEv"])
    rec.op("vector", lambda v: v.memset(gconst[:, 4:5], -1.0), inc=S["vEv"])
    rec.op("vector", lambda v: v.memset(gconst[:, 5:6], -LN128), inc=S["vEv"])
    ident_ready = S["vEv"].v




    p0ctx.close()

    a2s = ExitStack()
    K_sb = a2s.enter_context(nc.sbuf_tensor("K_sb", [128, HKV * T], BF16))
    V_sb = a2s.enter_context(nc.sbuf_tensor("V_sb", [128, TC * KVD], BF16))

    a1 = ExitStack()
    x_fm = a1.enter_context(nc.sbuf_tensor("x_fm", [128, CC * T], BF16))
    wbig = a1.enter_context(nc.sbuf_tensor("wbig", [128, CC * 512], BF16))
    wq_sb = [a1.enter_context(nc.sbuf_tensor(f"wq{i}", [128, CC * 2 * D], BF16)) for i in range(2)]
    qtile = [a1.enter_context(nc.sbuf_tensor("qt0", [128, T], BF16))] * 2
    gtmp = a1.enter_context(nc.sbuf_tensor("gtmp", [8, T], F32, side="right"))
    gbuf = a1.enter_context(nc.sbuf_tensor("gbuf", [8, T], F32))
    a2h_fm = gtmp
    a2q_fm = wq_sb[1][:, :].bitcast(F32)[0:8, :]

    xtm = [wbig[:, 0:C], wbig[:, C:2 * C]]
    xtm_marks = {}
    for tc in range(TC):
        w = [(S["tTr"], xtm_marks[tc - 2])] if tc >= 2 else []
        rec.op("gpsimd", lambda g, tc=tc: g.dma_start(
            out=xtm[tc % 2], in_=x_in[tc * 128:(tc + 1) * 128, :]),
            waits=w, inc=S["dmaX"], incn=16)
        ld_mark = S["dmaX"].v
        for cc in range(CC):
            i = (tc * CC + cc) % 8
            ps_free(i)
            w2 = [(S["dmaX"], ld_mark)] if cc == 0 else []
            if tc == 0 and cc == 0:
                w2.append((S["vEv"], ident_ready))
            rec.op("tensor", lambda t, tc=tc, cc=cc, i=i: t.matmul(
                pool[i][:, 0:128], xtm[tc % 2][:, cc * 128:(cc + 1) * 128],
                ident_bf[:, :], start=True, stop=True),
                waits=w2, inc=S["tTr"])
            rec.op("vector", lambda v, tc=tc, cc=cc, i=i: v.tensor_copy(
                x_fm[:, cc * T + tc * 128: cc * T + (tc + 1) * 128], pool[i][:, 0:128]),
                waits=[(S["tTr"], S["tTr"].v)], inc=ps_rel(i))
        xtm_marks[tc] = S["tTr"].v
    rec.op("vector", lambda v: v.engine_nop(), inc=S["vEv"])
    xfm_ready = S["vEv"].v
    def cast_dma(dst, src):
        rec.op("gpsimd", lambda g, d=dst, s=src: g.dma_start(out=d, in_=s),
               inc=S["dmaC"], incn=16)
    cast_dma(bUUU[:, 0 * RK:1 * RK, :], Ug[:, :, :])
    cast_dma(bUUU[:, 1 * RK:2 * RK, :], Uu[:, :, :])
    cast_dma(bUUU[:, 2 * RK:3 * RK, :], Ud[:, :, :])
    cast_dma(bCg[:, :, :], Cg[:, :, :])
    cast_dma(bCu[:, :, :], Cu[:, :, :])
    cast_dma(bCd[:, :, :], Cd[:, :, :])
    cast_dma(bRg[:, :], Rg[:, :])
    cast_dma(bRu[:, :], Ru[:, :])
    cast_dma(bRd[:, :], Rd[:, :])
    cast_dma(bgw[:, :], gw[:, :])
    cast_dma(bsRg[:, :], sRg[:, :])
    cast_dma(bsRu[:, :], sRu[:, :])
    cast_dma(bsRd[:, :], sRd[:, :])
    cast_dma(bsUUU[0 * RK:1 * RK, :], sUg[:, :])
    cast_dma(bsUUU[1 * RK:2 * RK, :], sUu[:, :])
    cast_dma(bsUUU[2 * RK:3 * RK, :], sUd[:, :])
    cast_dma(bsCg[:, :], sCg[:, :])
    cast_dma(bsCu[:, :], sCu[:, :])
    cast_dma(bsCd[:, :], sCd[:, :])
    cast_total = S["dmaC"].v


    for kh in range(2):
        rec.op("gpsimd", lambda g, kh=kh: g.dma_start(
            out=wbig[:, :].rearrange("p (cc f) -> p cc f", f=512),
            in_=Wk[:, kh * 512:(kh + 1) * 512].rearrange("(cc p) f -> p cc f", p=128)),
            waits=([(S["tKV"], S["tKV"].v)] if kh > 0
                   else [(S["tTr"], xtm_marks[TC - 1])]),
            inc=S["dmaW"], incn=16)
        wk_done = S["dmaW"].v
        for mm_ in range(4):
            m = kh * 4 + mm_
            q0 = 4 * (m % 2)
            for i in range(q0, q0 + 4):
                ps_free(i)
            for cc in range(CC):
                stat = wbig[:, cc * 512 + mm_ * D: cc * 512 + (mm_ + 1) * D]
                mm512([pool[q0], pool[q0 + 1], pool[q0 + 2], pool[q0 + 3]], stat,
                      x_fm[:, cc * T:(cc + 1) * T],
                      start=(cc == 0), stop=(cc == CC - 1),
                      waits=([(S["dmaW"], wk_done), (S["vEv"], xfm_ready)] if cc == 0 else ()),
                      inc=(S["tKV"] if cc == CC - 1 else None))
            for i in range(4):
                rec.op("vector", lambda v, m=m, i=i, q0=q0: v.tensor_copy(
                    K_sb[:, m * T + i * 512: m * T + (i + 1) * 512], pool[q0 + i][:, :]),
                    waits=([(S["tKV"], S["tKV"].v)] if i == 0 else ()), inc=ps_rel(q0 + i))

    for fh in range(2):
        rec.op("gpsimd", lambda g, fh=fh: g.dma_start(
            out=wbig[:, :].rearrange("p (cc f) -> p cc f", f=512),
            in_=Wv[:, fh * 512:(fh + 1) * 512].rearrange("(cc p) f -> p cc f", p=128)),
            waits=[(S["tKV"], S["tKV"].v)], inc=S["dmaW"], incn=16)
        wv_done = S["dmaW"].v
        for tc in range(TC):
            i = tc % 6
            ps_free(i)
            for cc in range(CC):
                stat = x_fm[:, cc * T + tc * 128: cc * T + (tc + 1) * 128]
                rec.op("tensor", lambda t, i=i, stat=stat, cc=cc: t.matmul(
                    pool[i][:, :], stat, wbig[:, cc * 512:(cc + 1) * 512],
                    start=(cc == 0), stop=(cc == CC - 1)),
                    waits=([(S["dmaW"], wv_done)] if cc == 0 else ()),
                    inc=(S["tKV"] if cc == CC - 1 else None))
            rec.op("vector", lambda v, tc=tc, fh=fh, i=i: v.tensor_copy(
                V_sb[:, tc * KVD + fh * 512: tc * KVD + (fh + 1) * 512], pool[i][:, :]),
                waits=[(S["tKV"], S["tKV"].v)], inc=ps_rel(i))

    rec.op("gpsimd", lambda g: g.dma_start(
        out=wq_sb[0][:, 0:CC * 8].rearrange("p (cc h) -> p cc h", h=8),
        in_=Wg.rearrange("(cc p) h -> p cc h", p=128)),
        inc=S["dmaW"], incn=16)
    wg_done = S["dmaW"].v
    for i in range(4, 8):
        ps_free(i)
    for cc in range(CC):
        stat = wq_sb[0][:, cc * 8:(cc + 1) * 8]
        mm512([pool[4], pool[5], pool[6], pool[7]], stat,
              x_fm[:, cc * T:(cc + 1) * T],
              start=(cc == 0), stop=(cc == CC - 1),
              waits=([(S["dmaW"], wg_done)] if cc == 0 else ()),
              inc=(S["tG"] if cc == CC - 1 else None), rows=8)
    for i in range(4):
        rec.op("vector", lambda v, i=i: v.tensor_copy(
            gtmp[:, i * 512:(i + 1) * 512], pool[4 + i][0:8, :]),
            waits=([(S["tG"], S["tG"].v)] if i == 0 else ()),
            inc=ps_rel(4 + i))
    rec.op("vector", lambda v: v.engine_nop(), inc=S["vEv"])
    g_ev = S["vEv"].v
    rec.op("scalar", lambda s: s.nop(), waits=[(S["vEv"], g_ev)])
    warm(AF.Sigmoid)
    rec.op("scalar", lambda s: s.activation(gbuf[:, :], gtmp[:, :], AF.Sigmoid,
                                            bias=gconst[0:8, 0:1]), inc=S["sFF"])
    warm(AF.Ln)
    rec.op("scalar", lambda s: s.activation(gtmp[:, :], gbuf[:, :], AF.Ln), inc=S["sFF"])
    lg_done = S["sFF"].v
    rec.op("vector", lambda v: v.tensor_tensor_scan(gbuf[:, :], gtmp[:, :], gtmp[:, :],
                                                    0.0, ALU.add, ALU.bypass),
           waits=[(S["sFF"], lg_done)], inc=S["vEv"])
    gam_done = S["vEv"].v
    rec.op("scalar", lambda s: s.nop(), waits=[(S["vEv"], gam_done)])
    warm(AF.Exp)
    rec.op("scalar", lambda s: s.activation(a2h_fm[:, :], gbuf[:, :], AF.Exp,
                                            scale=gconst[0:8, 1:2], bias=gconst[0:8, 2:3]),
           inc=S["sFF"])
    rec.op("scalar", lambda s: s.activation(egam[:, :], gbuf[:, :], AF.Exp), inc=S["sFF"])
    rec.op("scalar", lambda s: s.activation(a2q_fm[:, :], gbuf[:, :], AF.Exp,
                                            scale=gconst[0:8, 4:5], bias=gconst[0:8, 5:6]),
           inc=S["sFF"])
    a2h_done = S["sFF"].v
    for sc in range(TC):
        i = sc % 4
        ps_free(i)
        rec.op("tensor", lambda t, sc=sc, i=i: t.matmul(
            pool[i][:, 0:8], a2h_fm[:, sc * 128:(sc + 1) * 128], ident[0:8, 0:8],
            start=True, stop=True),
            waits=([(S["sFF"], a2h_done), (S["vEv"], ident_ready)] if sc == 0 else ()),
            inc=S["tTr"])
        rec.op("tensor", lambda t, sc=sc, i=i: t.matmul(
            pool[i][:, 8:16], a2q_fm[:, sc * 128:(sc + 1) * 128], ident[0:8, 0:8],
            start=True, stop=True), inc=S["tTr"])
        rec.op("vector", lambda v, sc=sc, i=i: v.tensor_copy(
            a2h_tm[:, sc * 8:(sc + 1) * 8], pool[i][:, 0:8]),
            waits=[(S["tTr"], S["tTr"].v)])
        rec.op("vector", lambda v, sc=sc, i=i: v.tensor_copy(
            a2q_tm[:, sc * 8:(sc + 1) * 8], pool[i][:, 8:16]), inc=ps_rel(i))

    for p in range(HQ // 2):
        rec.op("gpsimd", lambda g, p=p: g.dma_start(
            out=wq_sb[p % 2][:, :].rearrange("p (cc f) -> p cc f", f=2 * D),
            in_=Wq[:, p * 2 * D:(p + 1) * 2 * D].rearrange("(cc p) f -> p cc f", p=128)),
            waits=([(S["tQ"], 2 * (p - 1))] if p >= 2
                   else [(S["tG"], S["tG"].v), (S["tTr"], S["tTr"].v)]),
            inc=S["dmaW"], incn=16)
        wq_done = S["dmaW"].v
        for h in range(2):
            j = 2 * p + h
            q0 = 4 * (j % 2)
            for i in range(q0, q0 + 4):
                ps_free(i)
            for cc in range(CC):
                stat = wq_sb[p % 2][:, cc * 2 * D + h * D: cc * 2 * D + (h + 1) * D]
                mm512([pool[q0], pool[q0 + 1], pool[q0 + 2], pool[q0 + 3]], stat,
                      x_fm[:, cc * T:(cc + 1) * T],
                      start=(cc == 0), stop=(cc == CC - 1),
                      waits=([(S["dmaW"], wq_done)] if cc == 0 else ()),
                      inc=(S["tQ"] if cc == CC - 1 else None))
            for i in range(4):
                w = []
                if i == 0:
                    w.append((S["tQ"], S["tQ"].v))
                    if j >= 1:
                        w.append((S["dmaQ"], 16 * j))
                rec.op("vector", lambda v, j=j, i=i, q0=q0: v.tensor_copy(
                    qtile[j % 2][:, i * 512:(i + 1) * 512], pool[q0 + i][:, :]),
                    waits=w, inc=ps_rel(q0 + i))
            rec.op("vector", lambda v: v.engine_nop(), inc=S["vEv"])
            rec.op("sync", lambda sy, j=j: sy.dma_start(out=qsp[j, :, :],
                                                        in_=qtile[j % 2][:, :]),
                   waits=[(S["vEv"], S["vEv"].v)], inc=S["dmaQ"], incn=16)

    dump_sbuf("K", K_sb[:, :], waits=[(S["tKV"], S["tKV"].v)])
    dump_sbuf("V", V_sb[:, :], waits=[(S["tKV"], S["tKV"].v)])
    dump_sbuf("gam", gbuf[:, :], waits=[(S["sFF"], S["sFF"].v)])
    dump_sbuf("egam", egam[:, :], waits=[(S["sFF"], S["sFF"].v)])
    dump_sbuf("a2h_tm", a2h_tm[:, :], waits=[(S["vEv"], S["vEv"].v)])
    a1.close()
    bar()

    selctx = ExitStack()
    sel = selctx.enter_context(nc.sbuf_tensor("sel", [32, 32 * 128], BF16, side="right"))
    rec.op("gpsimd", lambda g: g.iota(sel[:, :], pattern=[[-1, 32], [0, 128]],
                                      base=0, channel_multiplier=1,
                                      allow_small_or_imprecise_dtypes=True),
           inc=S["dmaC"])
    rec.op("vector", lambda v: v.tensor_single_scalar(sel[:, :], sel[:, :], 0.0,
                                                      ALU.is_equal),
           waits=[(S["dmaC"], S["dmaC"].v)], inc=S["vEv"])
    rec.wait("vector", S["vEv"], S["vEv"].v)
    sel_ready = S["vEv"].v

    a_mid = ExitStack()
    ysb = a_mid.enter_context(nc.sbuf_tensor("ysb", [128, HQ * T], BF16, side="right"))
    pool67A.close()
    a2q = ExitStack()
    qpair = [a2q.enter_context(nc.sbuf_tensor(f"qp{i}", [128, 2 * T], BF16))
             for i in range(2)]
    psS = [a2q.enter_context(nc.psum_tensor(f"psS{i}", [128, 1024], F32))
           for i in range(2)]
    asb = [a2q.enter_context(nc.sbuf_tensor(f"asb{i}", [128, 1024], BF16, side="right"))
           for i in range(2)]
    norm_stage = a2q.enter_context(nc.sbuf_tensor("norm_stage", [1, 1024], F32))

    warm(AF.Square)
    nglob = 0
    diag_cnt = 0
    y_done = {}
    chunk_id = {}
    for m in range(HKV):
        rec.op("sync", lambda sy, m=m: sy.dma_start(out=qpair[m % 2][:, 0:T],
                                                    in_=qsp[2 * m, :, :]),
               waits=([(S["tQf"], m - 1)] if m >= 2 else ()), inc=S["dmaR"], incn=16)
        rec.op("sync", lambda sy, m=m: sy.dma_start(out=qpair[m % 2][:, T:2 * T],
                                                    in_=qsp[2 * m + 1, :, :]),
               inc=S["dmaR"], incn=16)
        qld = S["dmaR"].v
        for j in range(NSTRIP):
            nch = 4 * (j + 1)
            pend = []

            def emit_ynorm(info, last):
                c, par, wY, gidx = info
                vstat = V_sb[:, c * KVD + m * D: c * KVD + (m + 1) * D]
                if c == 0:
                    ps_free(4); ps_free(5); ps_free(0); ps_free(1)
                rec.op("tensor", lambda t, par=par, vstat=vstat, c=c, last=last: t.matmul(
                    pool[4][:, :], vstat, asb[par][:, 0:512],
                    start=(c == 0), stop=last), waits=wY)
                rec.op("tensor", lambda t, par=par, vstat=vstat, c=c, last=last: t.matmul(
                    pool[5][:, :], vstat, asb[par][:, 512:1024],
                    start=(c == 0), stop=last))
                rec.op("tensor", lambda t, par=par, c=c, last=last: t.matmul(
                    pool[0][0:1, :], ones_col[:, :], asb[par][:, 0:512],
                    start=(c == 0), stop=last))
                rec.op("tensor", lambda t, par=par, c=c, last=last: t.matmul(
                    pool[1][0:1, :], ones_col[:, :], asb[par][:, 512:1024],
                    start=(c == 0), stop=last), inc=S["tYn"])
                y_done[gidx] = S["tYn"].v

            for c in range(nch):
                par = nglob % 2
                kstat = K_sb[:, m * T + c * 128: m * T + (c + 1) * 128]
                w = [(S["sAe"], max(0, nglob - 1))]
                if c == 0:
                    w.append((S["dmaR"], qld))
                rec.op("tensor", lambda t, m=m, j=j, par=par, kstat=kstat: t.matmul(
                    psS[par][:, 0:512], kstat,
                    qpair[m % 2][:, j * 512:(j + 1) * 512], start=True, stop=True),
                    waits=w)
                rec.op("tensor", lambda t, m=m, j=j, par=par, kstat=kstat: t.matmul(
                    psS[par][:, 512:1024], kstat,
                    qpair[m % 2][:, T + j * 512: T + (j + 1) * 512], start=True, stop=True),
                    inc=S["tS"])
                wS = [(S["tS"], S["tS"].v),
                      (S["tYn"], y_done.get(nglob - 2, 0))]
                a2col = a2h_tm[:, c * 8 + m: c * 8 + m + 1]
                rec.op("scalar", lambda s, par=par, a2col=a2col: s.activation(
                    asb[par][:, :], psS[par][:, :], AF.Square, scale=a2col),
                    waits=wS, inc=S["sAe"])
                if c >= 4 * j:
                    kp = c - 4 * j
                    msk = band[:, 384 - 128 * kp: 896 - 128 * kp]
                    rec.op("vector", lambda v, par=par, msk=msk: v.tensor_mul(
                        asb[par][:, 0:512], asb[par][:, 0:512], msk),
                        waits=[(S["sAe"], S["sAe"].v)])
                    rec.op("vector", lambda v, par=par, msk=msk: v.tensor_mul(
                        asb[par][:, 512:1024], asb[par][:, 512:1024], msk),
                        inc=S["vAm"])
                    diag_cnt += 1
                    wY = [(S["vAm"], diag_cnt)]
                else:
                    wY = [(S["sAe"], S["sAe"].v)]
                pend.append((c, par, wY, nglob))
                if len(pend) > 1:
                    emit_ynorm(pend.pop(0), last=False)
                nglob += 1
            emit_ynorm(pend.pop(0), last=True)
            rec.op("vector", lambda v, m=m, j=j: v.tensor_copy(
                ysb[:, 2 * m * T + j * 512: 2 * m * T + (j + 1) * 512], pool[4][:, :]),
                waits=[(S["tYn"], S["tYn"].v)], inc=ps_rel(4))
            rec.op("vector", lambda v, m=m, j=j: v.tensor_copy(
                ysb[:, (2 * m + 1) * T + j * 512: (2 * m + 1) * T + (j + 1) * 512],
                pool[5][:, :]), inc=ps_rel(5))
            nsp = 2 * (NSTRIP * m + j)
            if nsp >= 2:
                rec.wait("vector", S["dmaQ"], 16 * HQ + 16 * nsp)
            rec.op("vector", lambda v: v.tensor_copy(
                norm_stage[0:1, 0:512], pool[0][0:1, :]), inc=ps_rel(0))
            rec.op("vector", lambda v: v.tensor_copy(
                norm_stage[0:1, 512:1024], pool[1][0:1, :]), inc=ps_rel(1))
            rec.op("vector", lambda v: v.engine_nop(), inc=S["vEv"])
            rec.op("gpsimd", lambda g, m=m, j=j: g.dma_start(
                out=normsp[2 * m:2 * m + 1, j * 512:(j + 1) * 512],
                in_=norm_stage[0:1, 0:512]),
                waits=[(S["vEv"], S["vEv"].v)], inc=S["dmaQ"], incn=16)
            rec.op("gpsimd", lambda g, m=m, j=j: g.dma_start(
                out=normsp[2 * m + 1:2 * m + 2, j * 512:(j + 1) * 512],
                in_=norm_stage[0:1, 512:1024]),
                inc=S["dmaQ"], incn=16)
        rec.op("tensor", lambda t: t.nop(), inc=S["tQf"])

    dump_sbuf("ypre", ysb[:, :], waits=[(S["vEv"], S["vEv"].v)])
    bar()
    a2q.close()
    a2s.close()
    pool67B = ExitStack()
    for i in (2, 3, 6, 7):
        pool[i] = pool67B.enter_context(nc.psum_tensor(f"pp{i}b", [128, 512], F32))

    a3b = ExitStack()
    wc_sb = a3b.enter_context(nc.sbuf_tensor("wc_sb", [128, CC * C], BF16))
    bc_sb = a3b.enter_context(nc.sbuf_tensor("bc_sb", [1, C], BF16))
    x_tile = [a3b.enter_context(nc.sbuf_tensor(f"xt{i}", [128, C], F32)) for i in range(2)]
    sq_tile = a3b.enter_context(nc.sbuf_tensor("sq", [128, C], F32))
    stat_sb = a3b.enter_context(nc.sbuf_tensor("stat_sb", [128, 8], F32))

    rec.op("gpsimd", lambda g: g.dma_start(
        out=wc_sb[:, :].rearrange("p (fc c) -> p fc c", c=C),
        in_=Wc.rearrange("(fc p) c -> p fc c", p=128)),
        inc=S["dmaW"], incn=16)
    rec.op("gpsimd", lambda g: g.dma_start(out=bc_sb[:, :], in_=bc_in[:, :]),
           inc=S["dmaW"], incn=16)
    wc_done = S["dmaW"].v

    a3a = ExitStack()
    norm_all = a3a.enter_context(nc.sbuf_tensor("norm_all", [16, T], F32))
    eg16 = a3a.enter_context(nc.sbuf_tensor("eg16", [16, T], F32))
    fac16 = norm_all
    fac16b = a3a.enter_context(nc.sbuf_tensor("fac16b", [16, T], BF16))

    rec.op("sync", lambda sy: sy.dma_start(out=norm_all[:, :], in_=normsp[:, :]),
           waits=[(S["dmaQ"], S["dmaQ"].v)], inc=S["dmaR"], incn=16)
    nall_ld = S["dmaR"].v
    for k in range(4):
        i = k % 2
        ps_free(i)
        rec.op("tensor", lambda t, k=k, i=i: t.matmul(
            pool[i][0:16, :], dup8[:, :], egam[:, k * 512:(k + 1) * 512],
            start=True, stop=True), inc=S["tTr"])
        rec.op("vector", lambda v, k=k, i=i: v.tensor_copy(
            eg16[:, k * 512:(k + 1) * 512], pool[i][0:16, :]),
            waits=[(S["tTr"], S["tTr"].v)], inc=ps_rel(i))
    rec.wait("vector", S["dmaR"], nall_ld)
    rec.wait("vector", prel[0], puse[0])
    rec.wait("vector", prel[1], puse[1])
    rec.op("vector", lambda v: v.tensor_mul(fac16[:, :], norm_all[:, :], eg16[:, :]))
    rec.op("vector", lambda v: v.engine_nop(), inc=S["vFx"])
    rec.wait("vector", S["vFx"], S["vFx"].v)
    rec.op("vector", lambda v: v.tensor_single_scalar(fac16[:, :], fac16[:, :], 1e-6,
                                                      ALU.add))
    rec.op("vector", lambda v: v.reciprocal(fac16[:, :], fac16[:, :]))
    rec.op("vector", lambda v: v.tensor_mul(fac16[:, :], fac16[:, :], eg16[:, :]))
    rec.op("vector", lambda v: v.tensor_copy(fac16b[:, :], fac16[:, :]), inc=S["vFx"])
    fac_done = S["vFx"].v

    for j in range(HQ):
        i0 = (j % 2) * 4
        for i in range(i0, i0 + 4):
            ps_free(i)
        mm512([pool[i0], pool[i0 + 1], pool[i0 + 2], pool[i0 + 3]],
              sel[0:16, j * 128:(j + 1) * 128], fac16b[:, :], start=True, stop=True,
              waits=[(S["vFx"], fac_done), (S["vEv"], sel_ready)], inc=S["tTr"])
        for i in range(4):
            rec.op("vector", lambda v, i=i, i0=i0, j=j: v.tensor_mul(
                ysb[:, j * T + i * 512: j * T + (i + 1) * 512],
                ysb[:, j * T + i * 512: j * T + (i + 1) * 512], pool[i0 + i][:, :]),
                waits=([(S["tTr"], S["tTr"].v)] if i == 0 else ()), inc=ps_rel(i0 + i))
        rec.op("vector", lambda v: v.engine_nop(), inc=S["vFx"])
    ynorm_done = S["vFx"].v
    dump_sbuf("fac16", fac16[:, :], waits=[(S["vFx"], ynorm_done)])
    dump_sbuf("ypost", ysb[:, :], waits=[(S["vFx"], ynorm_done)])
    dump_sbuf("nall", norm_all[:, :], waits=[(S["vFx"], ynorm_done)])
    a3a.close()

    warm(AF.Square)
    warm(AF.Ln)
    warm(AF.Exp)
    out_marks = {}
    for tc in range(TC):
        w = [(S["sLN"], S["sLN"].v)]
        if tc >= 2:
            w.append((S["dmaF"], 32 * (tc - 1)))
            if stage == "A" and (tc - 2) in out_marks:
                w.append((S["dmaO"], out_marks[tc - 2]))
        rec.op("sync", lambda sy, tc=tc: sy.dma_start(out=x_tile[tc % 2][:, :],
                                                      in_=x_in[tc * 128:(tc + 1) * 128, :]),
               waits=w, inc=S["dmaR"], incn=16)
        xre = S["dmaR"].v
        for half in range(2):
            hidx = tc * 2 + half
            i0 = (hidx % 3) * 2
            ps_free(i0); ps_free(i0 + 1)
            for fc in range(CC):
                stat = ysb[:, fc * T + tc * 128: fc * T + (tc + 1) * 128]
                w2 = [(S["vFx"], ynorm_done), (S["dmaW"], wc_done)] if fc == 0 else ()
                mm512([pool[i0], pool[i0 + 1]], stat,
                      wc_sb[:, fc * C + half * 1024: fc * C + (half + 1) * 1024],
                      start=(fc == 0), stop=False, waits=w2)
            rec.op("tensor", lambda t, i0=i0, half=half: t.matmul(
                pool[i0][:, :], ones_row[:, :], bc_sb[:, half * 1024:half * 1024 + 512],
                start=False, stop=True))
            rec.op("tensor", lambda t, i0=i0, half=half: t.matmul(
                pool[i0 + 1][:, :], ones_row[:, :],
                bc_sb[:, half * 1024 + 512:(half + 1) * 1024],
                start=False, stop=True), inc=S["tO"])
            for k in range(2):
                w3 = [(S["tO"], S["tO"].v)]
                if k == 0 and half == 0:
                    w3.append((S["dmaR"], xre))
                sl = slice(half * 1024 + k * 512, half * 1024 + (k + 1) * 512)
                rec.op("vector", lambda v, tc=tc, k=k, i0=i0, sl=sl: v.tensor_add(
                    x_tile[tc % 2][:, sl], pool[i0 + k][:, :], x_tile[tc % 2][:, sl]),
                    waits=w3, inc=ps_rel(i0 + k))
        zt = x_tile[tc % 2]
        rec.op("vector", lambda v, zt=zt: v.reduce_sum(stat_sb[:, 0:1], zt[:, :], axis=AX.X),
               inc=S["vLN"])
        if "z1" in dump:
            if "z1" not in dbg:
                dbg["z1"] = nc.dram_tensor("dbg_z1", [T, C], F32,
                                           kind="ExternalOutput").ap()
            rec.op("sync", lambda sy, tc=tc, zt=zt: sy.dma_start(
                out=dbg["z1"][tc * 128:(tc + 1) * 128, :], in_=zt[:, :]),
                waits=[(S["vLN"], S["vLN"].v)], inc=S["dmaO"], incn=16)
            rec.wait("vector", S["dmaO"], S["dmaO"].v)
        rec.op("scalar", lambda s, zt=zt: s.activation(sq_tile[:, :], zt[:, :], AF.Square),
               waits=[(S["vLN"], S["vLN"].v)], inc=S["sLN"])
        def vf(make):
            rec.op("vector", make, inc=S["vLN"])
            rec.wait("vector", S["vLN"], S["vLN"].v)
        rec.wait("vector", S["sLN"], S["sLN"].v)
        vf(lambda v: v.reduce_sum(stat_sb[:, 1:2], sq_tile[:, :], axis=AX.X))
        vf(lambda v: v.tensor_scalar_mul(stat_sb[:, 2:3], stat_sb[:, 0:1], 1.0 / C))
        vf(lambda v: v.tensor_scalar_mul(stat_sb[:, 3:4], stat_sb[:, 1:2], 1.0 / C))
        vf(lambda v: v.tensor_mul(stat_sb[:, 4:5], stat_sb[:, 2:3], stat_sb[:, 2:3]))
        vf(lambda v: v.tensor_sub(stat_sb[:, 4:5], stat_sb[:, 3:4], stat_sb[:, 4:5]))
        vf(lambda v: v.tensor_single_scalar(stat_sb[:, 4:5], stat_sb[:, 4:5], 1e-5,
                                            ALU.add))
        vf(lambda v: v.reciprocal(stat_sb[:, 5:6], stat_sb[:, 4:5]))
        rec.op("scalar", lambda s: s.activation(stat_sb[:, 6:7], stat_sb[:, 5:6], AF.Ln),
               waits=[(S["vLN"], S["vLN"].v)], inc=S["sLN"])
        rec.wait("scalar", S["sLN"], S["sLN"].v)
        rec.op("scalar", lambda s: s.activation(stat_sb[:, 6:7], stat_sb[:, 6:7], AF.Exp,
                                                scale=gconst[:, 3:4]), inc=S["sLN"])
        rec.wait("vector", S["sLN"], S["sLN"].v)
        vf(lambda v: v.tensor_mul(stat_sb[:, 7:8], stat_sb[:, 2:3], stat_sb[:, 6:7]))
        vf(lambda v: v.tensor_scalar_mul(stat_sb[:, 7:8], stat_sb[:, 7:8], -1.0))
        rec.op("vector", lambda v, zt=zt: v.tensor_scalar(
            zt[:, :], zt[:, :], stat_sb[:, 6:7], stat_sb[:, 7:8], ALU.mult, ALU.add),
            inc=S["sLN"])
        xf_done = S["sLN"].v
        if "st1" in dump:
            if "st1" not in dbg:
                dbg["st1"] = nc.dram_tensor("dbg_st1", [T, 8], F32,
                                            kind="ExternalOutput").ap()
            rec.op("sync", lambda sy, tc=tc: sy.dma_start(
                out=dbg["st1"][tc * 128:(tc + 1) * 128, :], in_=stat_sb[:, :]),
                waits=[(S["sLN"], xf_done)], inc=S["dmaO"], incn=16)
            rec.wait("vector", S["dmaO"], S["dmaO"].v)
        rec.op("sync", lambda sy, tc=tc, zt=zt: sy.dma_start(
            out=xf32[tc * 128:(tc + 1) * 128, :], in_=zt[:, :]),
            waits=[(S["sLN"], xf_done)], inc=S["dmaF"], incn=16)
        rec.op("gpsimd", lambda g, tc=tc, zt=zt: g.dma_start(
            out=xfbf[:, tc * 128:(tc + 1) * 128, :].transpose([1, 0, 2]),
            in_=zt[:, :].rearrange("p (cc f) -> p cc f", f=128)),
            waits=[(S["sLN"], xf_done)], inc=S["dmaF"], incn=16)
        if stage == "A":
            rec.op("sync", lambda sy, tc=tc, zt=zt: sy.dma_start(
                out=out_t[tc * 128:(tc + 1) * 128, :], in_=zt[:, :]),
                waits=[(S["sLN"], xf_done)], inc=S["dmaO"], incn=16)
            out_marks[tc] = S["dmaO"].v
    xfall_done = S["dmaF"].v

    if stage == "A":
        rec.wait("sync", S["dmaO"], S["dmaO"].v)
        rec.flush(blockH)
        a3b.close()
        a_mid.close()
        selctx.close()
        pool67B.close()
        ctx.close()
        return nc

    bar()
    a3b.close()
    a_mid.close()

    b_outer = ExitStack()
    rg_fm = b_outer.enter_context(nc.sbuf_tensor("rg_fm", [128, T], BF16))
    ru_fm = b_outer.enter_context(nc.sbuf_tensor("ru_fm", [128, T], BF16))
    b2s = b_outer.enter_context(nc.sbuf_tensor("b2s", [128, T], BF16))
    sCdt = b_outer.enter_context(nc.sbuf_tensor("sCdt", [128, C], BF16))
    wef = b_outer.enter_context(nc.sbuf_tensor("wef", [32, T], F32))
    wefb = b_outer.enter_context(nc.sbuf_tensor("wefb", [32, T], BF16))
    Rdt = b_outer.enter_context(nc.sbuf_tensor("Rdt", [128, I_FF], BF16))

    b0 = ExitStack()
    xf_fm = b0.enter_context(nc.sbuf_tensor("xf_fm", [128, CC * T], BF16))
    Rgt = b0.enter_context(nc.sbuf_tensor("Rgt", [128, C], BF16))
    Rut = b0.enter_context(nc.sbuf_tensor("Rut", [128, C], BF16))
    gwt = b0.enter_context(nc.sbuf_tensor("gwt", [128, CC * E], BF16))
    sRgt = b0.enter_context(nc.sbuf_tensor("sRgt", [128, C], BF16))
    sRut = b0.enter_context(nc.sbuf_tensor("sRut", [128, C], BF16))
    sRdt = b0.enter_context(nc.sbuf_tensor("sRdt", [128, SI], BF16))
    sUt = b0.enter_context(nc.sbuf_tensor("sUt", [128, 384], BF16))
    sCgt = b0.enter_context(nc.sbuf_tensor("sCgt", [128, SI], BF16))
    sCut = b0.enter_context(nc.sbuf_tensor("sCut", [128, SI], BF16))
    p0_sb = b0.enter_context(nc.sbuf_tensor("p0_sb", [128, 32], F32))
    p1_sb = b0.enter_context(nc.sbuf_tensor("p1_sb", [128, 32], F32))
    tk_sb = b0.enter_context(nc.sbuf_tensor("tk_sb", [128, 32], F32))
    mx_sb = b0.enter_context(nc.sbuf_tensor("mx_sb", [128, 4], F32))
    a1g = b0.enter_context(nc.sbuf_tensor("a1g", [128, 512], BF16))
    a1u = b0.enter_context(nc.sbuf_tensor("a1u", [128, 512], BF16))
    a2g = b0.enter_context(nc.sbuf_tensor("a2g", [128, 512], BF16))
    a2u = b0.enter_context(nc.sbuf_tensor("a2u", [128, 512], BF16))
    gs_sb = [b0.enter_context(nc.sbuf_tensor(f"gs{i}", [128, 512], BF16)) for i in range(3)]
    hs_sb = [b0.enter_context(nc.sbuf_tensor(f"hs{i}", [128, 512], BF16)) for i in range(3)]
    rd_sb = b0.enter_context(nc.sbuf_tensor("rd_sb", [128, 512], BF16))

    def tload(dst, src, extra=()):
        w = [(S["dmaC"], cast_total)] + list(extra)
        rec.op("sync", lambda sy, d=dst, s=src: sy.dma_start(out=d, in_=s, transpose=True),
               waits=w, inc=S["dmaS"], incn=16)

    for cc in range(CC):
        tload(Rgt[:, cc * 128:(cc + 1) * 128], bRg[:, cc * 128:(cc + 1) * 128])
        tload(Rut[:, cc * 128:(cc + 1) * 128], bRu[:, cc * 128:(cc + 1) * 128])
    rr_done = S["dmaS"].v
    for cc in range(CC):
        tload(gwt[:, cc * E:(cc + 1) * E], bgw[:, cc * 128:(cc + 1) * 128])
    gwt_done = S["dmaS"].v
    for cc in range(CC):
        tload(sRgt[:, cc * 128:(cc + 1) * 128], bsRg[:, cc * 128:(cc + 1) * 128])
        tload(sRut[:, cc * 128:(cc + 1) * 128], bsRu[:, cc * 128:(cc + 1) * 128])
    tload(sUt[:, :], bsUUU[:, :])
    tload(sCgt[:, :], bsCg[:, :])
    tload(sCut[:, :], bsCu[:, :])
    sh_done = S["dmaS"].v
    for icb in range(SIC):
        tload(sRdt[:, icb * 128:(icb + 1) * 128], bsRd[:, icb * 128:(icb + 1) * 128])
    tload(sCdt[:, :], bsCd[:, :])
    for icb in range(IC):
        tload(Rdt[:, icb * 128:(icb + 1) * 128], bRd[:, icb * 128:(icb + 1) * 128])
    wt_done = S["dmaS"].v

    xfstg = [b0.enter_context(nc.sbuf_tensor(f"xfs{i}", [128, C], BF16)) for i in range(2)]
    xfs_marks = {}
    for cc in range(CC):
        w = [(S["dmaF"], xfall_done)] if cc == 0 else []
        if cc >= 2:
            w.append((S["tTr"], xfs_marks[cc - 2]))
        rec.op("sync", lambda sy, cc=cc: sy.dma_start(
            out=xfstg[cc % 2][:, :].rearrange("p (tc f) -> p tc f", f=128),
            in_=xfbf[cc, :, :].rearrange("(tc p) f -> p tc f", p=128)),
            waits=w, inc=S["dmaR"], incn=16)
        ld_mark = S["dmaR"].v
        for tc in range(TC):
            i = (cc * TC + tc) % 8
            ps_free(i)
            w2 = [(S["dmaR"], ld_mark)] if tc == 0 else []
            rec.op("tensor", lambda t, tc=tc, cc=cc, i=i: t.matmul(
                pool[i][:, 0:128], xfstg[cc % 2][:, tc * 128:(tc + 1) * 128],
                ident_bf[:, :], start=True, stop=True),
                waits=w2, inc=S["tTr"])
            rec.op("vector", lambda v, tc=tc, cc=cc, i=i: v.tensor_copy(
                xf_fm[:, cc * T + tc * 128: cc * T + (tc + 1) * 128], pool[i][:, 0:128]),
                waits=[(S["tTr"], S["tTr"].v)], inc=ps_rel(i))
        xfs_marks[cc] = S["tTr"].v
    rec.op("vector", lambda v: v.engine_nop(), inc=S["vEv"])
    xffm_ready = S["vEv"].v
    dump_sbuf("xffm", xf_fm[:, :], waits=[(S["vEv"], xffm_ready)])

    for tgt, wsrc in ((rg_fm, Rgt), (ru_fm, Rut)):
        for i in range(4):
            ps_free(i)
        for cc in range(CC):
            stat = wsrc[:, cc * 128:(cc + 1) * 128]
            mm512([pool[0], pool[1], pool[2], pool[3]], stat,
                  xf_fm[:, cc * T:(cc + 1) * T],
                  start=(cc == 0), stop=(cc == CC - 1),
                  waits=([(S["dmaS"], rr_done), (S["vEv"], xffm_ready)]
                         if cc == 0 else ()),
                  inc=(S["tKV"] if cc == CC - 1 else None))
        for i in range(4):
            rec.op("vector", lambda v, tgt=tgt, i=i: v.tensor_copy(
                tgt[:, i * 512:(i + 1) * 512], pool[i][:, :]),
                waits=([(S["tKV"], S["tKV"].v)] if i == 0 else ()), inc=ps_rel(i))

    warm(AF.Exp)
    for tc in range(TC):
        if tc >= 1:
            rec.wait("vector", S["tTr"], S["tTr"].v)
        i = 4 + tc % 2
        ps_free(i)
        for cc in range(CC):
            stat = xf_fm[:, cc * T + tc * 128: cc * T + (tc + 1) * 128]
            rec.op("tensor", lambda t, i=i, stat=stat, cc=cc: t.matmul(
                pool[i][:, 0:32], stat, gwt[:, cc * E:(cc + 1) * E],
                start=(cc == 0), stop=(cc == CC - 1)),
                waits=([(S["dmaS"], gwt_done)] if (tc == 0 and cc == 0) else ()),
                inc=(S["tKV"] if cc == CC - 1 else None))
        tl_done = S["tKV"].v
        rec.op("vector", lambda v, i=i: v.reduce_max(mx_sb[:, 0:1], pool[i][:, 0:32],
                                                     axis=AX.X),
               waits=[(S["tKV"], tl_done)], inc=S["vTk"])
        rec.wait("vector", S["vTk"], S["vTk"].v)
        rec.op("vector", lambda v: v.tensor_scalar_mul(mx_sb[:, 1:2], mx_sb[:, 0:1], -1.0),
               inc=S["vTk"])
        rec.op("scalar", lambda s, i=i: s.activation(p0_sb[:, :], pool[i][:, 0:32], AF.Exp,
                                                     bias=mx_sb[:, 1:2]),
               waits=[(S["vTk"], S["vTk"].v)], inc=S["sFF"])
        pr = ps_rel(i)
        rec.op("vector", lambda v: v.tensor_copy(p1_sb[:, :], p0_sb[:, :]),
               waits=[(S["sFF"], S["sFF"].v)], inc=pr)
        rec.wait("vector", prel[i], puse[i])
        def vfk(make):
            rec.op("vector", make, inc=S["vTk"])
            rec.wait("vector", S["vTk"], S["vTk"].v)
        for r in range(4):
            vfk(lambda v: v.reduce_max(mx_sb[:, 2:3], p1_sb[:, :], axis=AX.X))
            if r < 3:
                vfk(lambda v: v.tensor_single_scalar(
                    tk_sb[:, :], p1_sb[:, :], mx_sb[:, 2:3], ALU.is_equal))
                vfk(lambda v: v.scalar_tensor_tensor(
                    tk_sb[:, :], p1_sb[:, :], 2.0, tk_sb[:, :], ALU.add, ALU.mult))
                vfk(lambda v: v.tensor_sub(p1_sb[:, :], p1_sb[:, :], tk_sb[:, :]))
        vfk(lambda v: v.tensor_single_scalar(tk_sb[:, :], p0_sb[:, :],
                                             mx_sb[:, 2:3], ALU.is_ge))
        vfk(lambda v: v.tensor_mul(tk_sb[:, :], tk_sb[:, :], p0_sb[:, :]))
        vfk(lambda v: v.reduce_sum(mx_sb[:, 3:4], tk_sb[:, :], axis=AX.X))
        vfk(lambda v: v.tensor_single_scalar(mx_sb[:, 3:4], mx_sb[:, 3:4], 1e-20, ALU.add))
        vfk(lambda v: v.reciprocal(mx_sb[:, 3:4], mx_sb[:, 3:4]))
        rec.op("vector", lambda v: v.tensor_single_scalar(tk_sb[:, :], tk_sb[:, :],
                                                          mx_sb[:, 3:4], ALU.mult),
               inc=S["vTk"])
        wtk = S["vTk"].v
        i2 = 6 + tc % 2
        ps_free(i2)
        rec.op("tensor", lambda t, i2=i2: t.matmul(
            pool[i2][0:32, 0:128], tk_sb[:, :], ident[:, :], start=True, stop=True),
            waits=[(S["vTk"], wtk)], inc=S["tTr"])
        rec.op("vector", lambda v, tc=tc, i2=i2: v.tensor_copy(
            wef[:, tc * 128:(tc + 1) * 128], pool[i2][0:32, 0:128]),
            waits=[(S["tTr"], S["tTr"].v)], inc=ps_rel(i2))
    rec.op("vector", lambda v: v.tensor_copy(wefb[:, :], wef[:, :]), inc=S["vTk"])

    rec.op("scalar", lambda s: s.nop(), waits=[(S["vTk"], S["vTk"].v)])
    warm(AF.Silu)

    for ts in range(4):
        for tag, wsrc, dst, i in (("g", sRgt, a1g, 6), ("u", sRut, a1u, 7)):
            ps_free(i)
            for cc in range(CC):
                stat = wsrc[:, cc * 128:(cc + 1) * 128]
                rec.op("tensor", lambda t, i=i, stat=stat, cc=cc, ts=ts: t.matmul(
                    pool[i][:, :], stat,
                    xf_fm[:, cc * T + ts * 512: cc * T + (ts + 1) * 512],
                    start=(cc == 0), stop=(cc == CC - 1)),
                    waits=([(S["dmaS"], sh_done)] if (ts == 0 and cc == 0) else ()),
                    inc=(S["tKV"] if cc == CC - 1 else None))
            rec.op("vector", lambda v, dst=dst, i=i: v.tensor_copy(dst[:, :], pool[i][:, :]),
                   waits=[(S["tKV"], S["tKV"].v)], inc=ps_rel(i))
        rec.op("vector", lambda v: v.engine_nop(), inc=S["vEv"])
        a1_done = S["vEv"].v
        for tag, srcb, dst, ucol, i in (("g", a1g, a2g, 0, 6), ("u", a1u, a2u, 1, 7)):
            ps_free(i)
            rec.op("tensor", lambda t, i=i, srcb=srcb, ucol=ucol: t.matmul(
                pool[i][:, :], sUt[:, ucol * 128:(ucol + 1) * 128], srcb[:, :],
                start=True, stop=True),
                waits=[(S["vEv"], a1_done)], inc=S["tKV"])
            rec.op("vector", lambda v, dst=dst, i=i: v.tensor_copy(dst[:, :], pool[i][:, :]),
                   waits=[(S["tKV"], S["tKV"].v)], inc=ps_rel(i))
        rec.op("vector", lambda v: v.engine_nop(), inc=S["vEv"])
        a2_done = S["vEv"].v
        ps_free(6)
        h_done = {}

        def rd_mm_s(jj, ts=None):
            wl = [(S["vEv"], h_done[jj])]
            if jj == 0:
                wl.append((S["dmaS"], wt_done))
            rec.op("tensor", lambda t, jj=jj: t.matmul(
                pool[6][:, :], sRdt[:, jj * 128:(jj + 1) * 128], hs_sb[jj % 3][:, :],
                start=(jj == 0), stop=(jj == SIC - 1)),
                waits=wl,
                inc=(S["tKV"] if jj == SIC - 1 else None))

        for icb in range(SIC):
            ig = icb % 3
            iu = 3 + icb % 3
            ps_free(ig); ps_free(iu)
            rec.op("tensor", lambda t, ig=ig, icb=icb: t.matmul(
                pool[ig][:, :], sCgt[:, icb * 128:(icb + 1) * 128], a2g[:, :],
                start=True, stop=True),
                waits=([(S["vEv"], a2_done)] if icb == 0 else ()), inc=S["tKV"])
            gdone = S["tKV"].v
            rec.op("tensor", lambda t, iu=iu, icb=icb: t.matmul(
                pool[iu][:, :], sCut[:, icb * 128:(icb + 1) * 128], a2u[:, :],
                start=True, stop=True), inc=S["tKV"])
            udone = S["tKV"].v
            rec.op("scalar", lambda s, ig=ig, icb=icb: s.activation(
                gs_sb[icb % 3][:, :], pool[ig][:, :], AF.Silu),
                waits=[(S["tKV"], gdone)], inc=ps_rel(ig))
            rec.op("scalar", lambda s: s.nop(), inc=S["sFF"])
            rec.op("vector", lambda v, iu=iu, icb=icb: v.tensor_mul(
                hs_sb[icb % 3][:, :], gs_sb[icb % 3][:, :], pool[iu][:, :]),
                waits=[(S["sFF"], S["sFF"].v), (S["tKV"], udone)], inc=ps_rel(iu))
            rec.op("vector", lambda v: v.engine_nop(), inc=S["vEv"])
            h_done[icb] = S["vEv"].v
            if icb >= 2:
                rd_mm_s(icb - 2)
        rd_mm_s(SIC - 2)
        rd_mm_s(SIC - 1)
        rec.op("vector", lambda v: v.tensor_copy(rd_sb[:, :], pool[6][:, :]),
               waits=[(S["tKV"], S["tKV"].v)], inc=ps_rel(6))
        rec.op("vector", lambda v: v.engine_nop(), inc=S["vEv"])
        ps_free(7)
        rec.op("tensor", lambda t: t.matmul(pool[7][:, :], sUt[:, 256:384], rd_sb[:, :],
                                            start=True, stop=True),
               waits=[(S["vEv"], S["vEv"].v)], inc=S["tKV"])
        rec.op("vector", lambda v, ts=ts: v.tensor_copy(
            b2s[:, ts * 512:(ts + 1) * 512], pool[7][:, :]),
            waits=[(S["tKV"], S["tKV"].v)], inc=ps_rel(7))

    dump_sbuf("rg", rg_fm[:, :], waits=[(S["vEv"], S["vEv"].v), (S["vTk"], S["vTk"].v)])
    dump_sbuf("ru", ru_fm[:, :], waits=[(S["vEv"], S["vEv"].v)])
    dump_sbuf("wef", wef[:, :], waits=[(S["vTk"], S["vTk"].v)])
    dump_sbuf("b2s", b2s[:, :], waits=[(S["vEv"], S["vEv"].v)])
    b0.close()
    bar()

    b1 = ExitStack()
    eUUU = [b1.enter_context(nc.sbuf_tensor(f"eU{i}", [128, 384], BF16)) for i in range(2)]
    eCg = [b1.enter_context(nc.sbuf_tensor(f"eCg{i}", [128, I_FF], BF16)) for i in range(2)]
    eCu = [b1.enter_context(nc.sbuf_tensor(f"eCu{i}", [128, I_FF], BF16)) for i in range(2)]
    eCd = [b1.enter_context(nc.sbuf_tensor(f"eCd{i}", [128, 512], BF16)) for i in range(4)]
    pg_sb = b1.enter_context(nc.sbuf_tensor("pg_sb", [128, 512], BF16))
    pu_sb = b1.enter_context(nc.sbuf_tensor("pu_sb", [128, 512], BF16))
    web_sb = b1.enter_context(nc.sbuf_tensor("web_sb", [128, 512], F32))
    pd_all = b1.enter_context(nc.sbuf_tensor("pd_all", [128, E * 512], BF16))
    ge_sb = [b1.enter_context(nc.sbuf_tensor(f"ge{i}", [128, 512], BF16)) for i in range(3)]
    he_sb = [b1.enter_context(nc.sbuf_tensor(f"he{i}", [128, 512], BF16)) for i in range(3)]
    rde_sb = b1.enter_context(nc.sbuf_tensor("rde_sb", [128, 512], BF16))
    z2_sb = [b1.enter_context(nc.sbuf_tensor(f"z2_{i}", [128, C], F32)) for i in range(4)]
    xfb_sb = [b1.enter_context(nc.sbuf_tensor(f"xfb{i}", [128, C], F32)) for i in range(4)]
    o_sb = [b1.enter_context(nc.sbuf_tensor(f"osb{i}", [128, C], F32)) for i in range(2)]
    sq2_sb = b1.enter_context(nc.sbuf_tensor("sq2", [128, C], F32))
    st2_sb = b1.enter_context(nc.sbuf_tensor("st2", [128, 8], F32))

    n_cd = 0
    for blk in range(NBLK):
        tb = blk * 512
        warm(AF.Silu)
        rec.wait("vector", S["tO"], S["tO"].v)
        for e in range(E):
            eb = e % 2
            ke = blk * E + e
            wprev = ([(S["tWf"], S["tWf"].v - 1)] if ke >= 2 else [])
            if blk == 0:
                wprev = wprev + [(S["dmaC"], cast_total)]
                if e >= 2:
                    wprev = wprev + [(S["bSt"], 48 * (e - 1))]
                rec.op("sync", lambda sy, e=e, eb=eb: sy.dma_start(
                    out=eUUU[eb][:, :], in_=bUUU[e, :, :], transpose=True),
                    waits=wprev, inc=S["dmaS"], incn=16)
                rec.op("sync", lambda sy, e=e, eb=eb: sy.dma_start(
                    out=eCg[eb][:, :], in_=bCg[e, :, :], transpose=True),
                    inc=S["dmaS"], incn=16)
                rec.op("sync", lambda sy, e=e, eb=eb: sy.dma_start(
                    out=eCu[eb][:, :], in_=bCu[e, :, :], transpose=True),
                    inc=S["dmaS"], incn=16)
                ew_done = S["dmaS"].v
                rec.op("gpsimd", lambda g, e=e, eb=eb: g.dma_start(
                    out=bUUUT[e, :, :], in_=eUUU[eb][:, :]),
                    waits=[(S["dmaS"], ew_done)], inc=S["bSt"], incn=16)
                rec.op("gpsimd", lambda g, e=e, eb=eb: g.dma_start(
                    out=bCgT[e, :, :], in_=eCg[eb][:, :]), inc=S["bSt"], incn=16)
                rec.op("gpsimd", lambda g, e=e, eb=eb: g.dma_start(
                    out=bCuT[e, :, :], in_=eCu[eb][:, :]), inc=S["bSt"], incn=16)
            else:
                rec.op("sync", lambda sy, e=e, eb=eb: sy.dma_start(
                    out=eUUU[eb][:, :], in_=bUUUT[e, :, :]),
                    waits=wprev + [(S["bSt"], 48 * E)], inc=S["dmaS"], incn=16)
                rec.op("sync", lambda sy, e=e, eb=eb: sy.dma_start(
                    out=eCg[eb][:, :], in_=bCgT[e, :, :]), inc=S["dmaS"], incn=16)
                rec.op("sync", lambda sy, e=e, eb=eb: sy.dma_start(
                    out=eCu[eb][:, :], in_=bCuT[e, :, :]), inc=S["dmaS"], incn=16)
                ew_done = S["dmaS"].v
            ps_free(6)
            rec.op("tensor", lambda t, eb=eb, tb=tb: t.matmul(
                pool[6][:, :], eUUU[eb][:, 0:128], rg_fm[:, tb:tb + 512],
                start=True, stop=True),
                waits=[(S["dmaS"], ew_done)], inc=S["tKV"])
            rec.op("vector", lambda v: v.tensor_copy(pg_sb[:, :], pool[6][:, :]),
                   waits=[(S["tKV"], S["tKV"].v)], inc=ps_rel(6))
            ps_free(7)
            rec.op("tensor", lambda t, eb=eb, tb=tb: t.matmul(
                pool[7][:, :], eUUU[eb][:, 128:256], ru_fm[:, tb:tb + 512],
                start=True, stop=True), inc=S["tKV"])
            rec.op("vector", lambda v: v.tensor_copy(pu_sb[:, :], pool[7][:, :]),
                   waits=[(S["tKV"], S["tKV"].v)], inc=ps_rel(7))
            ps_free(6)
            rec.op("tensor", lambda t, e=e, tb=tb: t.matmul(
                pool[6][:, :], sel[:, e * 128:(e + 1) * 128], wefb[:, tb:tb + 512],
                start=True, stop=True),
                waits=[(S["vTk"], S["vTk"].v)], inc=S["tKV"])
            rec.op("vector", lambda v: v.tensor_copy(web_sb[:, :], pool[6][:, :]),
                   waits=[(S["tKV"], S["tKV"].v)], inc=ps_rel(6))
            rec.op("vector", lambda v: v.engine_nop(), inc=S["vEv"])
            pgu_done = S["vEv"].v
            ps_free(6)
            h_done_e = {}

            def rd_mm_e(jj, _f=(blk == 0 and e == 0)):
                wl = [(S["vEv"], h_done_e[jj])]
                if jj == 0 and _f:
                    wl.append((S["dmaS"], wt_done))
                rec.op("tensor", lambda t, jj=jj: t.matmul(
                    pool[6][:, :], Rdt[:, jj * 128:(jj + 1) * 128], he_sb[jj % 3][:, :],
                    start=(jj == 0), stop=(jj == IC - 1)),
                    waits=wl,
                    inc=(S["tKV"] if jj == IC - 1 else None))

            for icb in range(IC):
                ig = icb % 3
                iu = 3 + icb % 3
                ps_free(ig); ps_free(iu)
                rec.op("tensor", lambda t, eb=eb, icb=icb, ig=ig: t.matmul(
                    pool[ig][:, :], eCg[eb][:, icb * 128:(icb + 1) * 128], pg_sb[:, :],
                    start=True, stop=True),
                    waits=([(S["vEv"], pgu_done)] if icb == 0 else ()), inc=S["tKV"])
                gdone = S["tKV"].v
                rec.op("tensor", lambda t, eb=eb, icb=icb, iu=iu: t.matmul(
                    pool[iu][:, :], eCu[eb][:, icb * 128:(icb + 1) * 128], pu_sb[:, :],
                    start=True, stop=True), inc=S["tKV"])
                udone = S["tKV"].v
                rec.op("scalar", lambda s, icb=icb, ig=ig: s.activation(
                    ge_sb[icb % 3][:, :], pool[ig][:, :], AF.Silu),
                    waits=[(S["tKV"], gdone)], inc=ps_rel(ig))
                rec.op("scalar", lambda s: s.nop(), inc=S["sFF"])
                rec.op("vector", lambda v, icb=icb, iu=iu: v.tensor_mul(
                    he_sb[icb % 3][:, :], ge_sb[icb % 3][:, :], pool[iu][:, :]),
                    waits=[(S["sFF"], S["sFF"].v), (S["tKV"], udone)], inc=ps_rel(iu))
                rec.op("vector", lambda v: v.engine_nop(), inc=S["vEv"])
                h_done_e[icb] = S["vEv"].v
                if icb >= 2:
                    rd_mm_e(icb - 2)
            rd_mm_e(IC - 2)
            rd_mm_e(IC - 1)
            rec.op("vector", lambda v: v.tensor_copy(rde_sb[:, :], pool[6][:, :]),
                   waits=[(S["tKV"], S["tKV"].v)], inc=ps_rel(6))
            rec.op("vector", lambda v: v.engine_nop(), inc=S["vEv"])
            ps_free(7)
            rec.op("tensor", lambda t, eb=eb: t.matmul(
                pool[7][:, :], eUUU[eb][:, 256:384], rde_sb[:, :], start=True, stop=True),
                waits=[(S["vEv"], S["vEv"].v)], inc=S["tWf"])
            rec.op("vector", lambda v, e=e: v.tensor_mul(
                pd_all[:, e * 512:(e + 1) * 512], pool[7][:, :], web_sb[:, :]),
                waits=[(S["tWf"], S["tWf"].v)], inc=ps_rel(7))
            rec.op("vector", lambda v: v.engine_nop(), inc=S["vPd"])
        pd_done = S["vPd"].v
        if "pd" in dump:
            if "pd" not in dbg:
                dbg["pd"] = nc.dram_tensor("dbg_pd", [NBLK, 128, E * 512], BF16,
                                           kind="ExternalOutput").ap()
            rec.op("sync", lambda sy, blk=blk: sy.dma_start(
                out=dbg["pd"][blk, :, :], in_=pd_all[:, :]),
                waits=[(S["vPd"], pd_done)], inc=S["dmaO"], incn=16)

        for tcl in range(4):
            tcg = blk * 4 + tcl
            w = []
            if blk > 0 and tcl == 0:
                w.append((S["sLN"], S["sLN"].v))
            rec.op("sync", lambda sy, tcg=tcg, tcl=tcl: sy.dma_start(
                out=xfb_sb[tcl][:, :], in_=xf32[tcg * 128:(tcg + 1) * 128, :]),
                waits=w, inc=S["dmaR"], incn=16)
        xfb_done = S["dmaR"].v
        for hq in range(4):
            tiles = [4 * (hq % 2) + i for i in range(4)]
            for e in range(E + 1):
                if e < E:
                    eb = n_cd % 4
                    n_cd += 1
                    wprev = ([(S["tCd"], n_cd - 4)] if n_cd >= 5 else [])
                    if blk == 0:
                        if n_cd >= 5:
                            wprev = wprev + [(S["bTl"], 16 * (n_cd - 4))]
                        rec.op("sync", lambda sy, e=e, eb=eb, hq=hq: sy.dma_start(
                            out=eCd[eb][:, :], in_=bCd[e, hq * 512:(hq + 1) * 512, :],
                            transpose=True),
                            waits=wprev, inc=S["dmaS"], incn=16)
                        cd_done = S["dmaS"].v
                        rec.op("gpsimd", lambda g, e=e, eb=eb, hq=hq: g.dma_start(
                            out=bCdT[e, hq, :, :], in_=eCd[eb][:, :]),
                            waits=[(S["dmaS"], cd_done)], inc=S["bTl"], incn=16)
                    else:
                        rec.op("sync", lambda sy, e=e, eb=eb, hq=hq: sy.dma_start(
                            out=eCd[eb][:, :], in_=bCdT[e, hq, :, :]),
                            waits=wprev + [(S["bTl"], 16 * 128)],
                            inc=S["dmaS"], incn=16)
                        cd_done = S["dmaS"].v
                for tcl in range(4):
                    i = tiles[tcl]
                    if e == 0:
                        ps_free(i)
                    if e < E:
                        stat = pd_all[:, e * 512 + tcl * 128: e * 512 + (tcl + 1) * 128]
                        mov = eCd[eb][:, :]
                        w4 = ([(S["dmaS"], cd_done), (S["vPd"], pd_done)] if tcl == 0 else ())
                    else:
                        stat = b2s[:, tb + tcl * 128: tb + (tcl + 1) * 128]
                        mov = sCdt[:, hq * 512:(hq + 1) * 512]
                        w4 = ()
                    inc = None
                    if e < E and tcl == 3:
                        inc = S["tCd"]
                    elif e == E and tcl == 3:
                        inc = S["tO"]
                    rec.op("tensor", lambda t, i=i, stat=stat, mov=mov, e=e: t.matmul(
                        pool[i][:, :], stat, mov, start=(e == 0), stop=(e == E)),
                        waits=w4, inc=inc)
            tO_now = S["tO"].v
            for tcl in range(4):
                i = tiles[tcl]
                w5 = [(S["tO"], tO_now)]
                if hq == 0 and tcl == 0:
                    w5.append((S["dmaR"], xfb_done))
                    w5.append((S["sLN"], S["sLN"].v))
                rec.op("vector", lambda v, tcl=tcl, hq=hq, i=i: v.tensor_add(
                    z2_sb[tcl][:, hq * 512:(hq + 1) * 512], pool[i][:, :],
                    xfb_sb[tcl][:, hq * 512:(hq + 1) * 512]),
                    waits=w5, inc=ps_rel(i))
        rec.op("vector", lambda v: v.engine_nop(), inc=S["vZ2"])
        z2_done = S["vZ2"].v
        if "z2" in dump:
            if "z2" not in dbg:
                dbg["z2"] = nc.dram_tensor("dbg_z2", [T, C], F32,
                                           kind="ExternalOutput").ap()
            for tcl in range(4):
                tcg = blk * 4 + tcl
                rec.op("sync", lambda sy, tcg=tcg, tcl=tcl: sy.dma_start(
                    out=dbg["z2"][tcg * 128:(tcg + 1) * 128, :], in_=z2_sb[tcl][:, :]),
                    waits=[(S["vZ2"], z2_done)], inc=S["dmaO"], incn=16)
            rec.wait("vector", S["dmaO"], S["dmaO"].v)

        warm(AF.Square)
        warm(AF.Ln)
        warm(AF.Exp)
        for tcl in range(4):
            tcg = blk * 4 + tcl
            rec.op("vector", lambda v, tcl=tcl: v.reduce_sum(
                st2_sb[:, 0:1], z2_sb[tcl][:, :], axis=AX.X),
                waits=[(S["vZ2"], z2_done)], inc=S["vLN"])
            rec.op("scalar", lambda s, tcl=tcl: s.activation(
                sq2_sb[:, :], z2_sb[tcl][:, :], AF.Square),
                waits=[(S["vLN"], S["vLN"].v)], inc=S["sLN"])
            def vf(make):
                rec.op("vector", make, inc=S["vLN"])
                rec.wait("vector", S["vLN"], S["vLN"].v)
            rec.wait("vector", S["sLN"], S["sLN"].v)
            vf(lambda v: v.reduce_sum(st2_sb[:, 1:2], sq2_sb[:, :], axis=AX.X))
            vf(lambda v: v.tensor_scalar_mul(st2_sb[:, 2:3], st2_sb[:, 0:1], 1.0 / C))
            vf(lambda v: v.tensor_scalar_mul(st2_sb[:, 3:4], st2_sb[:, 1:2], 1.0 / C))
            vf(lambda v: v.tensor_mul(st2_sb[:, 4:5], st2_sb[:, 2:3], st2_sb[:, 2:3]))
            vf(lambda v: v.tensor_sub(st2_sb[:, 4:5], st2_sb[:, 3:4], st2_sb[:, 4:5]))
            vf(lambda v: v.tensor_single_scalar(st2_sb[:, 4:5], st2_sb[:, 4:5], 1e-5,
                                                ALU.add))
            vf(lambda v: v.reciprocal(st2_sb[:, 5:6], st2_sb[:, 4:5]))
            rec.op("scalar", lambda s: s.activation(st2_sb[:, 6:7], st2_sb[:, 5:6], AF.Ln),
                   waits=[(S["vLN"], S["vLN"].v)], inc=S["sLN"])
            rec.wait("scalar", S["sLN"], S["sLN"].v)
            rec.op("scalar", lambda s: s.activation(st2_sb[:, 6:7], st2_sb[:, 6:7], AF.Exp,
                                                    scale=gconst[:, 3:4]), inc=S["sLN"])
            rec.wait("vector", S["sLN"], S["sLN"].v)
            vf(lambda v: v.tensor_mul(st2_sb[:, 7:8], st2_sb[:, 2:3], st2_sb[:, 6:7]))
            vf(lambda v: v.tensor_scalar_mul(st2_sb[:, 7:8], st2_sb[:, 7:8], -1.0))
            ob = tcg % 2
            w6 = []
            if tcg >= 2:
                w6.append((S["dmaO"], 16 * (tcg - 1)))
            rec.op("vector", lambda v, tcl=tcl, ob=ob: v.tensor_scalar(
                o_sb[ob][:, :], z2_sb[tcl][:, :], st2_sb[:, 6:7], st2_sb[:, 7:8],
                ALU.mult, ALU.add),
                waits=w6, inc=S["sLN"])
            rec.op("sync", lambda sy, tcg=tcg, ob=ob: sy.dma_start(
                out=out_t[tcg * 128:(tcg + 1) * 128, :], in_=o_sb[ob][:, :]),
                waits=[(S["sLN"], S["sLN"].v)], inc=S["dmaO"], incn=16)

    rec.wait("sync", S["dmaO"], S["dmaO"].v)
    rec.flush(blockH)
    b1.close()
    b_outer.close()
    selctx.close()
    pool67B.close()
    p0ctx.close()
    ctx.close()
    return nc


_PROG_CACHE = {}


def _get_prog(stage="full"):
    if stage not in _PROG_CACHE:
        _PROG_CACHE[stage] = build_core_program(stage)
    return _PROG_CACHE[stage]


def kernel(**inputs):
    nc = _get_prog("full")
    B = 8
    x = inputs["L0_attn_hidden_states"]
    base = {
        "Wq": inputs["L0_attn_Wq"], "Wk": inputs["L0_attn_Wk"], "Wv": inputs["L0_attn_Wv"],
        "Wg": inputs["L0_attn_Wg"], "Wc": inputs["L0_attn_Wc"],
        "bc": np.ascontiguousarray(np.asarray(inputs["L0_attn_bc"]).reshape(1, C)),
        "gw": inputs["L0_ffn_gate_w"],
        "Rg": inputs["L0_ffn_Rg"], "Ru": inputs["L0_ffn_Ru"], "Rd": inputs["L0_ffn_Rd"],
        "Ug": inputs["L0_ffn_Ug"], "Uu": inputs["L0_ffn_Uu"], "Ud": inputs["L0_ffn_Ud"],
        "Cg": inputs["L0_ffn_Cg"], "Cu": inputs["L0_ffn_Cu"], "Cd": inputs["L0_ffn_Cd"],
        "sRg": inputs["L0_ffn_s_Rg"], "sRu": inputs["L0_ffn_s_Ru"],
        "sRd": inputs["L0_ffn_s_Rd"],
        "sUg": inputs["L0_ffn_s_Ug"], "sUu": inputs["L0_ffn_s_Uu"],
        "sUd": inputs["L0_ffn_s_Ud"],
        "sCg": inputs["L0_ffn_s_Cg"], "sCu": inputs["L0_ffn_s_Cu"],
        "sCd": inputs["L0_ffn_s_Cd"],
    }
    base = {k: np.ascontiguousarray(np.asarray(v), dtype=np.float32)
            for k, v in base.items()}
    in_maps = []
    for c in range(B):
        m = dict(base)
        m["x"] = np.ascontiguousarray(np.asarray(x)[c], dtype=np.float32)
        in_maps.append(m)
    res = run_bass_kernel_spmd(nc, in_maps, core_ids=list(range(B)))
    return np.stack([np.asarray(r["out"]) for r in res.results], axis=0)
